# version 06
# baseline (speedup 1.0000x reference)
import contextlib
import math
import numpy as np

import concourse.bass as bass
import concourse.mybir as mybir
from concourse.bass_utils import run_bass_kernel_spmd

F32 = mybir.dt.float32
BF16 = mybir.dt.bfloat16
AF = mybir.ActivationFunctionType
ALU = mybir.AluOpType
AX = mybir.AxisListType

B, T, E = 2, 4096, 4096
HKV, HQ, D = 8, 32, 128
NSLOT, BOT, RDIM, TOPK, VB = 1024, 64, 48, 8, 256
EPS = 1e-5
TAU_LOW, TAU_HIGH, EARLY_EXIT = 0.5, 2.0, 0.05
NCH, NSL, OWN_CH = 32, 8, 8
BIGC = 1.0e30
HBIG = 5.0e29
ISQD = 1.0 / math.sqrt(D)

ENGINES = ("sync", "scalar", "vector", "tensor", "gpsimd")
_LAST_RES = None


class Sched:
    def __init__(self):
        self.q = {e: [] for e in ENGINES}
        self.cnt = {}
        self.waited = {e: {} for e in ENGINES}
        self.semnames = set(e + s for e in ENGINES for s in ("_c", "_d"))

    def op(self, eng, fn, deps=(), dma=False, sem=None):
        sname = sem if sem else (eng + ("_d" if dma else "_c"))
        self.semnames.add(sname)
        waits = {}
        for tok in deps:
            if tok is None:
                continue
            s, v = tok
            if v <= self.waited[eng].get(s, 0):
                continue
            waits[s] = max(waits.get(s, 0), v)
        for s, v in waits.items():
            self.waited[eng][s] = v
        amt = 16 if dma else 1
        self.cnt[sname] = self.cnt.get(sname, 0) + amt
        tok = (sname, self.cnt[sname])
        self.q[eng].append((sorted(waits.items()), fn, sname, amt))
        return tok

    def emit(self, block, sems):
        def mk(engname):
            def _(eng):
                for waits, fn, sname, amt in self.q[engname]:
                    for s, v in waits:
                        eng.wait_ge(sems[s], v)
                    fn(eng).then_inc(sems[sname], amt)
            return _
        for e in ENGINES:
            if self.q[e]:
                getattr(block, e)(mk(e))


DBG = False
ABL = set()


def build_nc():
    nc = bass.Bass("TRN2", target_bir_lowering=False)

    def di(name, shape):
        return nc.dram_tensor(name, shape, F32, kind="ExternalInput")

    x_own = di("x_own", [NCH, 128, E])
    x_oth = di("x_oth", [NCH, 128, E])
    wq_f = di("wq_f", [HQ, 128, E])
    wk_f = di("wk_f", [HKV, 128, E])
    wv_f = di("wv_f", [HKV, 128, E])
    wo_f = di("wo_f", [8, 128, 32 * 512])
    wu1_f = di("wu1_f", [8, 128, E])
    av_f = di("av_f", [32, 128, 8 * 128])
    wvd_f = di("wvd_f", [32, 128, VB])
    wvu_f = di("wvu_f", [8, 2, 128, 512])
    wqf_f = di("wqf_f", [128, 32 * BOT])
    wr_f = di("wr_f", [BOT, RDIM])
    ak_f = di("ak_f", [BOT, NSLOT])
    wu2_f = di("wu2_f", [128, 8])
    bu1_f = di("bu1_f", [128, 8])
    bu2_f = di("bu2_f", [1, 1])
    bo_f = di("bo_f", [1, E])
    lr_f = di("lr_f", [1, NSLOT])
    sc_f = di("sc_f", [1, 2])
    eye_f = di("eye_f", [128, 128])

    y_out = nc.dram_tensor("y", [OWN_CH, 128, E], F32, kind="ExternalOutput")
    if DBG:
        dbg_gate = nc.dram_tensor("dbg_gate", [128, OWN_CH], F32, kind="ExternalOutput")
        dbg_sgt = nc.dram_tensor("dbg_sgt", [128, OWN_CH], F32, kind="ExternalOutput")
        dbg_lv = nc.dram_tensor("dbg_lv", [128, 64], F32, kind="ExternalOutput")
        dbg_gs = nc.dram_tensor("dbg_gs", [1, 4], F32, kind="ExternalOutput")
        dbg_nm = nc.dram_tensor("dbg_nm", [128, 32], F32, kind="ExternalOutput")
        dbg_rs = nc.dram_tensor("dbg_rs", [128, 32], F32, kind="ExternalOutput")
        dbg_t1 = nc.dram_tensor("dbg_t1", [128, 2, 1024], F32, kind="ExternalOutput")
        dbg_q8 = nc.dram_tensor("dbg_q8", [65, 1024], F32, kind="ExternalOutput")
        dbg_wsum = nc.dram_tensor("dbg_wsum", [128, 2], F32, kind="ExternalOutput")
        dbg_yt = nc.dram_tensor("dbg_yt", [128, 1024], F32, kind="ExternalOutput")
        dbg_k = nc.dram_tensor("dbg_k", [HKV, 128, T], BF16, kind="ExternalOutput")
        dbg_v = nc.dram_tensor("dbg_v", [HKV, 128, T], BF16, kind="ExternalOutput")
        dbg_q = nc.dram_tensor("dbg_q", [HQ, 128, 1024], BF16, kind="ExternalOutput")
        dbg_xpx = nc.dram_tensor("dbg_xpx", [OWN_CH, 128, E], F32, kind="ExternalOutput")

    def dscr(name, shape, dt=BF16):
        return nc.dram_tensor(name, shape, dt, kind="Internal")

    s_wq = dscr("s_wq", [HQ, 128, E])
    s_wk = dscr("s_wk", [HKV, 128, E])
    s_wv = dscr("s_wv", [HKV, 128, E])
    s_wo = dscr("s_wo", [8, 128, 32 * 512])
    s_wu1 = dscr("s_wu1", [8, 128, E])
    s_av = dscr("s_av", [32, 128, 8 * 128])
    s_wvd = dscr("s_wvd", [32, 128, VB])
    s_wvu = dscr("s_wvu", [8, 2, 128, 512])
    s_bo = dscr("s_bo", [1, E])
    s_q = dscr("s_q", [HQ, 128, 1024])
    s_k = dscr("s_k", [HKV, 128, T])
    s_vd = dscr("s_vd", [HKV, 128, T])
    s_xpx = dscr("s_xpx", [OWN_CH, 128, E], F32)
    s_pw = dscr("s_pw", [OWN_CH, 128, 1024], BF16)
    d_den = dscr("d_den", [1, 512], F32)
    d_bc = dscr("d_bc", [1, 1], F32)

    es_ = contextlib.ExitStack()
    with es_:
        def sb(name, shape, dt=F32):
            return es_.enter_context(nc.sbuf_tensor(name, shape, dt))

        A = sb("A", [128, 8, E], BF16)
        V = sb("V", [128, 32, 1024], BF16)
        BGF = sb("BGF", [128, 5, 2048], F32)

        Vflat = V[:].rearrange("p a b -> p (a b)")
        XNT = Vflat[:, 0:16384].rearrange("p (a b) -> p a b", a=32, b=512)
        H1 = Vflat[:, 16384:24576].rearrange("p (a b) -> p a b", a=8, b=1024)

        def WTap(sc, c0, c1):
            reg = 6 if sc < 4 else 7
            base = (sc % 4) * 1024
            return A[:, reg, base + c0:base + c1]

        XB = [BGF[:, 0:2, :].rearrange("p a b -> p (a b)"),
              BGF[:, 2:4, :].rearrange("p a b -> p (a b)")]
        CAST0 = BGF[:, 4, 0:1024]
        CAST1 = BGF[:, 4, 1024:2048]
        SRT = BGF[:, 0, 0:1024]
        SRW = BGF[:, 0, 1024:2048]
        ACH = BGF[:, 1, 0:1024]
        TBF = BGF[:, 1, 1024:2048]

        Af = A[:].rearrange("p a b -> p (a b)")
        XCB = A[:, 4, :]
        AOT = A[:, 4, 1024:2048]
        WQF = A[:, 5, 1024:3072]
        STG_K = A[:, 5, 0:512]
        STG_V = A[:, 6, 0:512]
        STG_Q = A[:, 6, 512:1024]
        JUNK = A[:, 7, :]
        QB = [A[:, 6, 0:512], A[:, 6, 512:1024]]
        PT = [A[:, 6, 1024:1536], A[:, 6, 1536:2048],
              A[:, 6, 2048:2560], A[:, 6, 2560:3072]]
        WVU = A[:, 6, 2560:3584].rearrange("p (a b) -> p a b", a=2, b=512)
        BO1 = A[0:1, 7, 0:512]

        eyeb = sb("eyeb", [128, 128], BF16)
        ones_f = sb("ones_f", [128, 1])
        ones_b = sb("ones_b", [128, 1], BF16)
        ones1b = sb("ones1b", [1, 128], BF16)
        stat = sb("stat", [128, 16])
        negm = sb("negm", [128, NCH])
        rstd = sb("rstd", [128, NCH])
        a01 = sb("a01", [128, 2])
        lv_all = sb("lv_all", [128, 2 * NCH])
        gate = sb("gate", [128, OWN_CH])
        sgt = sb("sgt", [128, OWN_CH])
        gtmp = sb("gtmp", [128, 4])
        ginv_b = sb("ginv_b", [128, 1])
        gw1_b = sb("gw1_b", [128, 1])
        gb_b = sb("gb_b", [128, 1])
        gs_row = sb("gs_row", [1, 64])
        gs_one = sb("gs_one", [1, 4])
        dmat = sb("dmat", [128, 128], BF16)
        q8 = sb("q8", [BOT + 1, 1024], BF16)
        rq_a = sb("rq_a", [65, 1024], BF16)
        rk_a = sb("rk_a", [65, NSLOT], BF16)
        ak_a = sb("ak_a", [BOT + 1, NSLOT], BF16)
        wr_b = sb("wr_b", [BOT, RDIM], BF16)
        wu2_b = sb("wu2_b", [128, 8], BF16)
        bu1_s = sb("bu1_s", [128, 8])
        bu2_s = sb("bu2_s", [1, 1])
        t1 = sb("t1", [128, 2, 1024], BF16)
        wvd_b = sb("wvd_b", [128, 2, VB], BF16)
        wvu_t = sb("wvu_t", [128, 2, 512], BF16)
        bo1_t = sb("bo1_t", [1, 512], BF16)
        pw_t = sb("pw_t", [128, 1024], BF16)
        sgrow = sb("sgrow", [1, 1024], BF16)
        wsum = sb("wsum", [128, 2])
        winv = sb("winv", [128, 2])
        den_r = sb("den_r", [1, 512])
        den_b = sb("den_b", [128, 512])
        ostg = sb("ostg", [128, 2, 512])
        xpxs = sb("xpxs", [128, 2, 512])

        G = [es_.enter_context(nc.psum_tensor(f"g{i}", [128, 512], F32))
             for i in range(6)]
        PD = [es_.enter_context(nc.psum_tensor("pd0", [1, 512], F32)),
              es_.enter_context(nc.psum_tensor("pd1", [1, 512], F32))]

        sch = Sched()

        def dma(queue, out, in_, deps=(), sem=None):
            return sch.op(queue, lambda e, o=out, i=in_: e.dma_start(out=o, in_=i),
                          deps=deps, dma=True, sem=sem)

        def mm(out, lhsT, rhs, start, stop, deps=()):
            return sch.op("tensor",
                          lambda e, o=out, l=lhsT, r=rhs, s0=start, s1=stop:
                          e.matmul(o, l, r, start=s0, stop=s1), deps=deps)

        def vop(fn, deps=()):
            return sch.op("vector", fn, deps=deps)

        def sop(fn, deps=()):
            return sch.op("scalar", fn, deps=deps)

        gfree = [None] * 6
        pdfree = [None, None]

        t_eyf = dma("sync", BGF[:, 4, 0:128], eye_f[:], sem="d_eye")
        t_eyeb = vop(lambda e: e.tensor_copy(eyeb[:], BGF[:, 4, 0:128]), [t_eyf])
        t_onesf = vop(lambda e: e.memset(ones_f[:], 1.0))
        t_onesb = vop(lambda e: e.memset(ones_b[:], 1.0))
        t_ones1b = vop(lambda e: e.memset(ones1b[:], 1.0))
        t_gw1 = dma("gpsimd", gw1_b[:], sc_f[0:1, 0:1].to_broadcast([128, 1]), sem="d_gw1")
        t_gb = dma("gpsimd", gb_b[:], sc_f[0:1, 1:2].to_broadcast([128, 1]), sem="d_gb")
        t_bu1 = dma("sync", bu1_s[:], bu1_f[:], sem="d_bu1")
        t_bu2 = dma("sync", bu2_s[:], bu2_f[:], sem="d_bu2")

        t_wrf = dma("sync", BGF[0:BOT, 4, 128:176], wr_f[:], deps=[], sem="d_wr")
        t_wrb = vop(lambda e: e.tensor_copy(wr_b[:], BGF[0:BOT, 4, 128:176]), [t_wrf])
        t_akf = dma("sync", CAST1[0:BOT, :], ak_f[:], deps=[], sem="d_ak")
        t_akb = vop(lambda e: e.tensor_copy(ak_a[0:BOT, :], CAST1[0:BOT, :]), [t_akf])
        t_lrf = dma("sync", BGF[0:1, 4, 176:1200], lr_f[:], deps=[], sem="d_lr")
        t_lrb = vop(lambda e: e.tensor_copy(ak_a[BOT:BOT + 1, :], BGF[0:1, 4, 176:1200]),
                    [t_lrf])
        t_rqz = vop(lambda e: e.memset(rq_a[32:64, :], 0.0))
        t_rkz = vop(lambda e: e.memset(rk_a[32:64, :], 0.0))
        t_rklr = vop(lambda e: e.tensor_copy(rk_a[64:65, :],
                                             ak_a[BOT:BOT + 1, :]), [t_lrb, t_rkz])
        t_wu2f = dma("sync", BGF[:, 4, 1200:1208], wu2_f[:], deps=[], sem="d_wu2")
        t_wu2b = vop(lambda e: e.tensor_copy(wu2_b[:], BGF[:, 4, 1200:1208]), [t_wu2f])
        t_bof = dma("sync", XB[0][0:1, :], bo_f[:], deps=[], sem="d_boin")
        t_bob = vop(lambda e: e.tensor_copy(JUNK[0:1, 0:E], XB[0][0:1, :]), [t_bof])
        t_bos = dma("gpsimd", s_bo[:], JUNK[0:1, 0:E], deps=[t_bob], sem="d_bos")
        t_wqff = dma("sync", XB[1][:, 0:2048], wqf_f[:], deps=[], sem="d_wqf")
        t_wqfb = vop(lambda e: e.tensor_copy(WQF, XB[1][:, 0:2048]), [t_wqff])

        cast_jobs = []
        for j in range(HKV):
            cast_jobs.append((wk_f[j], s_wk[j], "wk", j))
        for j in range(HKV):
            cast_jobs.append((wv_f[j], s_wv[j], "wv", j))
        for j in range(HQ):
            cast_jobs.append((wq_f[j], s_wq[j], "wq", j))
        for j in range(8):
            cast_jobs.append((wu1_f[j], s_wu1[j], "wu1", j))
        for j in range(32):
            cast_jobs.append((av_f[j], s_av[j], "av", j))
        for j in range(32):
            cast_jobs.append((wvd_f[j], s_wvd[j], "wvd", j))
        for j in range(8):
            for rc in range(2):
                cast_jobs.append((wvu_f[j, rc], s_wvu[j, rc], "wvu", j))
        for j in range(8):
            cast_jobs.append((wo_f[j], s_wo[j], "wo", j))

        CASTB = [CAST0, CAST1]
        COUTB = [A[:, 0, 0:1024], A[:, 0, 1024:2048]]
        cin_free = [[t_eyeb, t_wrb], [t_akb, t_wu2b]]
        cout_free = [[], []]
        rdy = {}
        ci = 0
        for src, dst, tag, j in cast_jobs:
            total = src.shape[-2] * src.shape[-1] if len(src.shape) > 2 else None
            flat_src = src.rearrange("p f -> p f") if len(src.shape) == 2 else src
            free = src.shape[-1]
            ntile = (free + 1023) // 1024
            t_last = None
            for tt in range(ntile):
                f0 = tt * 1024
                f1 = min(free, f0 + 1024)
                w = f1 - f0
                bi = ci % 2
                ci += 1
                t_ld = dma("sync", CASTB[bi][:, 0:w], src[:, f0:f1], deps=cin_free[bi], sem=f"d_ci{bi}")
                if bi == 0:
                    t_c = sop(lambda e, o=COUTB[bi][:, 0:w], i=CASTB[bi][:, 0:w]:
                              e.copy(o, i), deps=[t_ld] + cout_free[bi])
                else:
                    t_c = vop(lambda e, o=COUTB[bi][:, 0:w], i=CASTB[bi][:, 0:w]:
                              e.tensor_copy(o, i), deps=[t_ld] + cout_free[bi])
                cin_free[bi] = [t_c]
                t_st = dma("gpsimd", dst[:, f0:f1], COUTB[bi][:, 0:w], deps=[t_c], sem=f"d_cst{bi}")
                cout_free[bi] = [t_st]
                t_last = t_st
            rdy[(tag, j)] = t_last
        t_wk_rdy = rdy[("wk", HKV - 1)]
        t_wv_rdy = rdy[("wv", HKV - 1)]
        t_wq_rdy = rdy[("wq", HQ - 1)]
        t_wu1_rdy = rdy[("wu1", 7)]
        t_av_rdy = rdy[("av", 31)]
        t_wvd_rdy = rdy[("wvd", 31)]
        t_wvu_rdy = rdy[("wvu", 7)]
        t_wo_rdy = rdy[("wo", 7)]


        t_lv = [None] * 64
        stat_free = [[], []]

        def chunk_stats(xb, t_ld, cc_lv, own_cc, prev_junk):
            si = (cc_lv % 2) * 8
            cs = [stat[:, si + k:si + k + 1] for k in range(8)]
            fr = stat_free[cc_lv % 2]
            t_sum = vop(lambda e: e.reduce_sum(cs[0], xb, axis=AX.X), [t_ld] + fr)
            t_sq2 = sop(lambda e: e.activation(JUNK, xb, AF.Square, accum_out=cs[2]),
                        [t_ld, prev_junk] + fr)
            t_mean = vop(lambda e: e.tensor_scalar(cs[1], cs[0], 1.0 / E, None,
                                                   ALU.mult), [t_sum])
            tn = tr = None
            if own_cc is not None:
                tn = vop(lambda e, cc=own_cc: e.tensor_scalar(
                    negm[:, cc:cc + 1], cs[1], -1.0, None, ALU.mult), [t_mean])
            t_msq = vop(lambda e: e.tensor_scalar(cs[3], cs[1], cs[1], None, ALU.mult),
                        [t_mean])
            t_var = vop(lambda e: e.scalar_tensor_tensor(
                cs[3], cs[2], 1.0 / E, cs[3], ALU.mult, ALU.subtract), [t_sq2, t_msq])
            t_veps = vop(lambda e: e.tensor_scalar(cs[4], cs[3], EPS, None, ALU.add),
                         [t_var])
            if own_cc is not None:
                t_sqr = sop(lambda e: e.activation(cs[5], cs[4], AF.Sqrt), [t_veps])
                tr = vop(lambda e, cc=own_cc: e.reciprocal(rstd[:, cc:cc + 1], cs[5]),
                         [t_sqr])
            if "lv" in ABL:
                tl = vop(lambda e, cc=cc_lv: e.memset(lv_all[:, cc:cc + 1], 0.5),
                         [])
                t_lv[cc_lv] = tl
                stat_free[cc_lv % 2] = [tl, t_veps] + ([tr] if tr else [])
                return tn, tr, tl, t_sq2
            t_rv = vop(lambda e: e.reciprocal(cs[6], cs[4]), [t_veps])
            t_vx = vop(lambda e: e.tensor_tensor(cs[7], cs[3], cs[6], ALU.mult),
                       [t_var, t_rv])
            tl = sop(lambda e, cc=cc_lv: e.activation(
                lv_all[:, cc:cc + 1], cs[7], AF.Ln, bias=1.0), [t_vx])
            t_lv[cc_lv] = tl
            stat_free[cc_lv % 2] = [tl] + ([tr] if tr else [])
            return tn, tr, tl, t_sq2

        xb_free = [[t_bob], [t_wqfb]]
        xcb_free = []
        dmat_free = []
        prev_junk = t_bos
        t_xpx_st = []
        t_q_st, t_k_st, t_v_st = [], [], []
        t_h1_ev = []
        t_q8_ev = []
        pan_free = [[], []]
        stgK_free, stgV_free, stgQ_free = [], [], []

        for sl in range(NSL):
            t_xnt_ev = []
            for j in range(4):
                cc = 4 * sl + j
                bi = cc % 2
                xb = XB[bi]
                t_ld = dma("sync", xb, x_own[cc],
                           deps=xb_free[bi], sem=f"d_xb{bi}")
                tn, tr, tl, prev_junk = chunk_stats(xb, t_ld, cc, cc, prev_junk)
                new_xb_free = [tl, prev_junk]
                t_xcb = vop(lambda e, xb=xb, cc=cc: e.tensor_scalar(
                    XCB, xb, negm[:, cc:cc + 1], None, ALU.add),
                    [t_ld, tn] + xcb_free)
                new_xb_free.append(t_xcb)
                if cc < OWN_CH and "xpx" not in ABL:
                    t_a1 = vop(lambda e, cc=cc: e.tensor_scalar(
                        a01[:, 1:2], rstd[:, cc:cc + 1], 1.0, None, ALU.add), [tr])
                    t_a0 = vop(lambda e, cc=cc: e.tensor_tensor(
                        a01[:, 0:1], negm[:, cc:cc + 1], rstd[:, cc:cc + 1],
                        ALU.mult), [tn, tr])
                    t_p1 = vop(lambda e, xb=xb: e.tensor_scalar(
                        xb, xb, a01[:, 1:2], None, ALU.mult),
                        [t_a1, t_xcb, tl, prev_junk])
                    t_p2 = vop(lambda e, xb=xb: e.tensor_scalar(
                        xb, xb, a01[:, 0:1], None, ALU.add), [t_a0, t_p1])
                    t_xst = dma("gpsimd", s_xpx[cc], xb,
                                deps=[t_p2], sem="d_xpxs")
                    t_xpx_st.append(t_xst)
                    new_xb_free = [t_xst]
                xb_free[bi] = new_xb_free
                t_dm = vop(lambda e, cc=cc: e.tensor_scalar(
                    dmat[:], eyeb[:], rstd[:, cc:cc + 1], None, ALU.mult),
                    [t_eyeb, tr] + dmat_free)
                tms = []
                for ec in range(32):
                    bnk = ec % 4
                    t_m = mm(G[bnk][:, 0:128], XCB[:, ec * 128:(ec + 1) * 128],
                             dmat[:], True, True, deps=[t_xcb, t_dm, gfree[bnk]])
                    dst = XNT[:, ec, j * 128:(j + 1) * 128]
                    src = G[bnk][:, 0:128]
                    if ec % 2 == 0:
                        t_e = sop(lambda e, d=dst, s=src: e.copy(d, s), [t_m])
                    else:
                        t_e = vop(lambda e, d=dst, s=src: e.tensor_copy(d, s), [t_m])
                    gfree[bnk] = t_e
                    tms.append(t_m)
                    t_xnt_ev.append(t_e)
                xcb_free = [tms[-1]]
                dmat_free = [tms[-1]]

            def proj32(pan, bnk, first_deps, out_ap=None):
                t_last = None
                oap = G[bnk][:, 0:512] if out_ap is None else out_ap
                for ec in range(32):
                    dep = first_deps if ec == 0 else []
                    t_last = mm(oap, pan[:, ec * 128:(ec + 1) * 128],
                                XNT[:, ec, :], ec == 0, ec == 31, deps=dep)
                return t_last

            for hc in range(HKV):
                bi = hc % 2
                pan = A[:, 2 + bi, :]
                t_pl = dma("sync", pan, s_wk[hc], deps=pan_free[bi] + [t_wk_rdy], sem=f"d_pan{bi}")
                bnk = 4 + (hc % 2)
                t_mmE = proj32(pan, bnk, [t_pl, gfree[bnk]] + t_xnt_ev)
                t_ev = sop(lambda e, b=bnk: e.copy(STG_K, G[b][:, 0:512]),
                           [t_mmE] + stgK_free)
                t_st = dma("gpsimd", s_k[hc][:, sl * 512:(sl + 1) * 512], STG_K,
                           deps=[t_ev], sem="d_kst")
                gfree[bnk] = t_ev
                stgK_free = [t_st]
                pan_free[bi] = [t_mmE]
                t_k_st.append(t_st)

            for hc in ([] if "vproj" in ABL else range(HKV)):
                bi = hc % 2
                pan = A[:, 2 + bi, :]
                t_pl = dma("sync", pan, s_wv[hc], deps=pan_free[bi] + [t_wv_rdy], sem=f"d_pan{bi}")
                bnk = 4 + (hc % 2)
                t_mmE = proj32(pan, bnk, [t_pl, gfree[bnk]] + t_xnt_ev)
                t_ev = sop(lambda e, b=bnk: e.copy(STG_V, G[b][:, 0:512]),
                           [t_mmE] + stgV_free)
                t_st = dma("gpsimd", s_vd[hc][:, sl * 512:(sl + 1) * 512], STG_V,
                           deps=[t_ev], sem="d_vst")
                gfree[bnk] = t_ev
                stgV_free = [t_st]
                pan_free[bi] = [t_mmE]
                t_v_st.append(t_st)

            if sl < 2:
                for qh in range(HQ):
                    bi = qh % 2
                    pan = A[:, 2 + bi, :]
                    t_pl = dma("sync", pan, s_wq[qh], deps=pan_free[bi] + [t_wq_rdy], sem=f"d_pan{bi}")
                    bnk = 4 + (qh % 2)
                    t_mmE = proj32(pan, bnk, [t_pl, gfree[bnk]] + t_xnt_ev)
                    t_ev = sop(lambda e, b=bnk: e.copy(STG_Q, G[b][:, 0:512]),
                               [t_mmE] + stgQ_free)
                    t_st = dma("gpsimd", s_q[qh][:, sl * 512:(sl + 1) * 512], STG_Q,
                               deps=[t_ev], sem="d_qst")
                    gfree[bnk] = t_ev
                    stgQ_free = [t_st]
                    pan_free[bi] = [t_mmE]
                    t_q_st.append(t_st)
                for uc in ([] if "wu1" in ABL else range(8)):
                    bi = uc % 2
                    pan = A[:, 2 + bi, :]
                    t_pl = dma("sync", pan, s_wu1[uc], deps=pan_free[bi] + [t_wu1_rdy], sem=f"d_pan{bi}")
                    bnk = 4 + (uc % 2)
                    t_mmE = proj32(pan, bnk, [t_pl, gfree[bnk]] + t_xnt_ev)
                    t_ev = sop(lambda e, b=bnk, uc=uc, sl=sl: e.activation(
                        H1[:, uc, sl * 512:(sl + 1) * 512], G[b][:, 0:512],
                        AF.Gelu_apprx_tanh, bias=bu1_s[:, uc:uc + 1]),
                        [t_mmE, t_bu1])
                    gfree[bnk] = t_ev
                    pan_free[bi] = [t_mmE]
                    t_h1_ev.append(t_ev)
                if "query" in ABL:
                    t_ev = vop(lambda e, sl=sl: e.memset(
                        q8[0:BOT, sl * 512:(sl + 1) * 512], 0.01), [])
                else:
                    bnk = 4
                    t_mmq = None
                    for ec in range(32):
                        dep = ([t_wqfb, gfree[bnk]] + t_xnt_ev) if ec == 0 else []
                        t_mmq = mm(G[bnk][0:BOT, 0:512],
                                   WQF[:, ec * BOT:(ec + 1) * BOT],
                                   XNT[:, ec, :], ec == 0, ec == 31, deps=dep)
                    t_ev = vop(lambda e, sl=sl, b=bnk: e.tensor_scalar(
                        q8[0:BOT, sl * 512:(sl + 1) * 512], G[b][0:BOT, 0:512],
                        0.125, None, ALU.mult), [t_mmq])
                    gfree[bnk] = t_ev
                t_q8_ev.append(t_ev)

            for j2 in range(4):
                cc2 = 4 * sl + j2
                bi = cc2 % 2
                xb = XB[bi]
                t_ld = dma("sync", xb, x_oth[cc2],
                           deps=xb_free[bi], sem=f"d_xb{bi}")
                _, _, tl2, prev_junk = chunk_stats(xb, t_ld, NCH + cc2, None,
                                                   prev_junk)
                xb_free[bi] = [tl2, prev_junk]


        pass

        bnk = 5
        t_gmm = mm(G[bnk][0:1, 0:64], ones_f[:], lv_all[:, 0:64], True, True,
                   deps=[t_onesf, gfree[bnk]] + t_lv)
        t_gcp = vop(lambda e, b=bnk: e.tensor_copy(gs_row[:], G[b][0:1, 0:64]), [t_gmm])
        gfree[bnk] = t_gcp
        t_gred = vop(lambda e: e.reduce_sum(gs_one[0:1, 0:1], gs_row[:], axis=AX.X),
                     [t_gcp])
        t_gm = vop(lambda e: e.tensor_scalar(gs_one[0:1, 1:2], gs_one[0:1, 0:1],
                                             1.0 / (B * T), 1e-6, ALU.mult,
                                             op1=ALU.add), [t_gred])
        t_gi = vop(lambda e: e.reciprocal(gs_one[0:1, 2:3], gs_one[0:1, 1:2]), [t_gm])
        t_gb0 = dma("gpsimd", d_bc[:], gs_one[0:1, 2:3], deps=[t_gi], sem="d_gbc")
        t_gbc = dma("gpsimd", ginv_b[:], d_bc[:].to_broadcast([128, 1]), deps=[t_gb0], sem="d_gbc")

        t_sgr = []
        for th in range(2):
            bnk = 5
            t_mml = None
            for uc in range(8):
                dep = ([t_wu2b, gfree[bnk]] + t_h1_ev) if uc == 0 else []
                t_mml = mm(G[bnk][0:1, 0:512], wu2_b[:, uc:uc + 1],
                           H1[:, uc, th * 512:(th + 1) * 512], uc == 0, uc == 7,
                           deps=dep)
            t_sgv = sop(lambda e, th=th, b=bnk: e.activation(
                sgrow[0:1, th * 512:(th + 1) * 512], G[b][0:1, 0:512],
                AF.Sigmoid, bias=bu2_s[0:1, 0:1]), [t_mml, t_bu2])
            gfree[bnk] = t_sgv
            t_sgr.append(t_sgv)
        t_sgt = []
        for tc in range(OWN_CH):
            bnk = 5
            t_mt = mm(G[bnk][0:128, 0:1], sgrow[0:1, tc * 128:(tc + 1) * 128],
                      ones_b[0:1, 0:1], True, True,
                      deps=[t_onesb, gfree[bnk]] + t_sgr)
            t_e = vop(lambda e, tc=tc, b=bnk: e.tensor_copy(
                sgt[:, tc:tc + 1], G[b][0:128, 0:1]), [t_mt])
            gfree[bnk] = t_e
            t_sgt.append(t_e)

        t_gate = []
        gtmp_free = []
        for tc in range(OWN_CH):
            g0 = gtmp[:, 0:1]
            g1 = gtmp[:, 1:2]
            t_nv = vop(lambda e, tc=tc: e.tensor_scalar(
                g0, lv_all[:, tc:tc + 1], ginv_b[:, 0:1], 0.5, ALU.mult,
                op1=ALU.mult), [t_gbc, t_lv[tc]] + gtmp_free)
            t_u = vop(lambda e, tc=tc: e.scalar_tensor_tensor(
                g1, sgt[:, tc:tc + 1], 2.5, g0, ALU.mult, ALU.add),
                [t_nv, t_sgt[tc]])
            t_u2 = vop(lambda e: e.tensor_scalar(g1, g1, 5.0, 0.0, ALU.min,
                                                 op1=ALU.max), [t_u])
            t_un = vop(lambda e: e.tensor_scalar(
                g1, g1, -TAU_LOW, 1.0 / (TAU_HIGH - TAU_LOW), ALU.add,
                op1=ALU.mult), [t_u2])
            t_un2 = vop(lambda e: e.tensor_scalar(g1, g1, 1.0, 0.0, ALU.min,
                                                  op1=ALU.max), [t_un])
            t_gl = vop(lambda e: e.tensor_scalar(g1, g1, gw1_b[:, 0:1], None,
                                                 ALU.mult), [t_un2, t_gw1])
            t_gl2 = vop(lambda e: e.tensor_scalar(g1, g1, gb_b[:, 0:1], None,
                                                  ALU.add), [t_gl, t_gb])
            t_sig = sop(lambda e: e.activation(g0, g1, AF.Sigmoid), [t_gl2])
            t_msk = vop(lambda e: e.tensor_scalar(g1, g0, EARLY_EXIT, None,
                                                  ALU.is_ge), [t_sig])
            t_g = vop(lambda e, tc=tc: e.tensor_tensor(
                gate[:, tc:tc + 1], g0, g1, ALU.mult), [t_msk])
            gtmp_free = [t_g]
            t_gate.append(t_g)

        t_q8o = vop(lambda e: e.memset(q8[BOT:BOT + 1, :], 1.0))
        t_rq = []
        for th in range(2):
            bnk = 5
            t_m = mm(G[bnk][0:RDIM, 0:512], wr_b[:],
                     q8[0:BOT, th * 512:(th + 1) * 512], True, True,
                     deps=[t_wrb, gfree[bnk]] + t_q8_ev)
            t_e = vop(lambda e, th=th, b=bnk: e.tensor_scalar(
                rq_a[0:RDIM, th * 512:(th + 1) * 512], G[b][0:RDIM, 0:512],
                8.0 / math.sqrt(RDIM), None, ALU.mult), [t_m, t_rqz])
            gfree[bnk] = t_e
            t_rq.append(t_e)
        t_rqo = vop(lambda e: e.memset(rq_a[64:65, :], 1.0), [t_rqz])
        t_rk = [t_rklr]
        for th in range(2):
            bnk = 5
            t_m = mm(G[bnk][0:RDIM, 0:512], wr_b[:],
                     ak_a[0:BOT, th * 512:(th + 1) * 512], True, True,
                     deps=[t_wrb, t_akb, gfree[bnk]])
            t_e = sop(lambda e, th=th, b=bnk: e.copy(
                rk_a[0:RDIM, th * 512:(th + 1) * 512], G[b][0:RDIM, 0:512]),
                [t_m, t_rkz])
            gfree[bnk] = t_e
            t_rk.append(t_e)

        ffn_state = {
            "pw_free": [], "srt_free": list(xb_free[0]), "ach_free": list(xb_free[0]),
            "tbf_free": list(xb_free[0]), "srw_free": list(xb_free[0]),
            "t_sr": [], "t_ach": [], "t_w": None, "t_m8": None,
        }

        def emit_ffn(i, ybnk, t_slot):
            fs = ffn_state
            tc, st = i // 8, i % 8
            c_m = gtmp[:, 2:3]
            bankfree = t_slot
            if st == 0:
                fs["t_sr"] = []
                fs["t_ach"] = []
                t_m = mm(G[ybnk][:, 0:512], rq_a[:, tc * 128:(tc + 1) * 128],
                         rk_a[:, 0:512], True, True,
                         deps=[t_rqo, t_slot] + t_rq + t_rk)
                t_e = sop(lambda e, b=ybnk: e.copy(SRT[:, 0:512], G[b][:, 0:512]),
                          [t_m] + fs["srt_free"])
                fs["srt_free"] = []
                fs["t_sr"].append(t_e)
                bankfree = t_e
            elif st == 1:
                t_m = mm(G[ybnk][:, 0:512], rq_a[:, tc * 128:(tc + 1) * 128],
                         rk_a[:, 512:1024], True, True,
                         deps=[t_rqo, t_slot] + t_rq + t_rk)
                t_e = sop(lambda e, b=ybnk: e.copy(SRT[:, 512:1024], G[b][:, 0:512]),
                          [t_m])
                fs["t_sr"].append(t_e)
                fs["t_w"] = vop(lambda e: e.tensor_copy(SRW[:], SRT[:]),
                                fs["t_sr"] + fs["srw_free"])
                fs["srw_free"] = []
                bankfree = t_e
            elif st == 2:
                t_m = mm(G[ybnk][:, 0:512], q8[:, tc * 128:(tc + 1) * 128],
                         ak_a[:, 0:512], True, True,
                         deps=[t_q8o, t_akb, t_lrb, t_slot] + t_q8_ev)
                t_e = sop(lambda e, b=ybnk: e.copy(ACH[:, 0:512], G[b][:, 0:512]),
                          [t_m] + fs["ach_free"])
                fs["ach_free"] = []
                fs["t_ach"].append(t_e)
                bankfree = t_e
                t_mx = vop(lambda e: e.reduce_max(c_m, SRW[:], axis=AX.X),
                           [fs["t_w"]])
                t_tb = vop(lambda e: e.tensor_scalar(
                    TBF[:], SRW[:], c_m, BIGC, ALU.is_lt, op1=ALU.mult),
                    [t_mx] + fs["tbf_free"])
                fs["tbf_free"] = []
                fs["t_w"] = vop(lambda e: e.scalar_tensor_tensor(
                    SRW[:], TBF[:], HBIG, SRW[:], ALU.subtract, ALU.min), [t_tb])
            elif st == 3:
                t_m = mm(G[ybnk][:, 0:512], q8[:, tc * 128:(tc + 1) * 128],
                         ak_a[:, 512:1024], True, True,
                         deps=[t_q8o, t_akb, t_lrb, t_slot] + t_q8_ev)
                t_e = sop(lambda e, b=ybnk: e.copy(ACH[:, 512:1024], G[b][:, 0:512]),
                          [t_m])
                fs["t_ach"].append(t_e)
                bankfree = t_e
                for _ in range(2):
                    t_mx = vop(lambda e: e.reduce_max(c_m, SRW[:], axis=AX.X),
                               [fs["t_w"]])
                    t_tb = vop(lambda e: e.tensor_scalar(
                        TBF[:], SRW[:], c_m, BIGC, ALU.is_lt, op1=ALU.mult), [t_mx])
                    fs["t_w"] = vop(lambda e: e.scalar_tensor_tensor(
                        SRW[:], TBF[:], HBIG, SRW[:], ALU.subtract, ALU.min), [t_tb])
            elif st in (4, 5):
                for _ in range(2):
                    t_mx = vop(lambda e: e.reduce_max(c_m, SRW[:], axis=AX.X),
                               [fs["t_w"]])
                    t_tb = vop(lambda e: e.tensor_scalar(
                        TBF[:], SRW[:], c_m, BIGC, ALU.is_lt, op1=ALU.mult), [t_mx])
                    fs["t_w"] = vop(lambda e: e.scalar_tensor_tensor(
                        SRW[:], TBF[:], HBIG, SRW[:], ALU.subtract, ALU.min), [t_tb])
            elif st == 6:
                t_mx = vop(lambda e: e.reduce_max(c_m, SRW[:], axis=AX.X),
                           [fs["t_w"]])
                t_tb = vop(lambda e: e.tensor_scalar(
                    TBF[:], SRW[:], c_m, BIGC, ALU.is_lt, op1=ALU.mult), [t_mx])
                t_w = vop(lambda e: e.scalar_tensor_tensor(
                    SRW[:], TBF[:], HBIG, SRW[:], ALU.subtract, ALU.min), [t_tb])
                t_m8 = vop(lambda e: e.reduce_max(c_m, SRW[:], axis=AX.X), [t_w])
                fs["srw_free"] = [t_m8]
                t_tb2 = vop(lambda e: e.tensor_scalar(
                    TBF[:], SRT[:], c_m, BIGC, ALU.is_lt, op1=ALU.mult),
                    [t_m8] + fs["t_sr"])
                fs["srt_free"] = [t_tb2]
                t_akn = vop(lambda e: e.scalar_tensor_tensor(
                    ACH[:], TBF[:], HBIG, ACH[:], ALU.subtract, ALU.min),
                    [t_tb2] + fs["t_ach"])
                fs["tbf_free"] = [t_akn]
                t_am = vop(lambda e: e.reduce_max(gtmp[:, 3:4], ACH[:], axis=AX.X),
                           [t_akn])
                fs["t_nam"] = vop(lambda e: e.tensor_scalar(
                    gtmp[:, 2:3], gtmp[:, 3:4], -1.0, None, ALU.mult), [t_am])
                fs["t_akn"] = t_akn
            else:
                wi = tc % 2
                t_exp = sop(lambda e, wi=wi: e.activation(
                    pw_t[:], ACH[:], AF.Exp, bias=gtmp[:, 2:3],
                    accum_out=wsum[:, wi:wi + 1]),
                    [fs["t_nam"], fs["t_akn"]] + fs["pw_free"])
                fs["ach_free"] = [t_exp]
                t_wi = vop(lambda e, wi=wi: e.reciprocal(
                    winv[:, wi:wi + 1], wsum[:, wi:wi + 1]), [t_exp])
                t_sc2 = vop(lambda e, wi=wi, tc=tc: e.tensor_tensor(
                    winv[:, wi:wi + 1], winv[:, wi:wi + 1], gate[:, tc:tc + 1],
                    ALU.mult), [t_wi, t_gate[tc]])
                t_pws = vop(lambda e, wi=wi: e.tensor_scalar(
                    pw_t[:], pw_t[:], winv[:, wi:wi + 1], None, ALU.mult),
                    [t_sc2, t_exp])
                t_pst = dma("gpsimd", s_pw[tc], pw_t[:], deps=[t_pws], sem="d_pw")
                fs["pw_free"] = [t_pst]
                fs["t_pw_st"] = fs.get("t_pw_st", []) + [t_pst]
            return bankfree

        kp_free = [cout_free[0] + cout_free[1], []]
        vp_free = [list(pan_free[0]), list(pan_free[1])]
        vk_free = [[], [t_k_st[-1]]]
        qb_free = [[t_v_st[-1] if t_v_st else None], [t_q_st[-1] if t_q_st else None]]
        pt_carry = [None, None, None, None]
        denb_free = []
        denr_free = []
        t_ye_all = []
        for hkv in range(HKV):
            bi = hkv % 2
            kp = A[:, 0 + bi, :]
            vp = A[:, 2 + bi, :]
            t_kl = dma("sync", kp, s_k[hkv], deps=kp_free[bi] + t_k_st, sem=f"d_kp{bi}")
            t_vl = dma("sync", vp, s_vd[hkv], deps=vp_free[bi] + t_v_st, sem=f"d_vp{bi}")
            vk = A[:, 4 + bi, :]
            t_vk_ev = []
            for kc in range(32):
                bnk = 2 if kc % 2 == 0 else 3
                t_m = mm(G[bnk][:, 0:128], vp[:, kc * 128:(kc + 1) * 128], eyeb[:],
                         True, True,
                         deps=[t_vl, t_eyeb, gfree[bnk]] + vk_free[bi])
                dst = vk[:, kc * 128:(kc + 1) * 128]
                src = G[bnk][:, 0:128]
                if kc % 2 == 0:
                    t_e = sop(lambda e, d=dst, s=src: e.copy(d, s), [t_m])
                else:
                    t_e = vop(lambda e, d=dst, s=src: e.tensor_copy(d, s), [t_m])
                gfree[bnk] = t_e
                t_vk_ev.append(t_e)
            vk_free[bi] = []
            vp_free[bi] = [t_vk_ev[-1]]
            t_last_av = None
            for g in range(4):
                h = hkv * 4 + g
                t_ql = [dma("sync", QB[qi], s_q[h][:, qi * 512:(qi + 1) * 512],
                            deps=qb_free[qi] + t_q_st, sem=f"d_qb{qi}")
                        for qi in range(2)]
                SBKs = [[0, 1], [4, 5]]
                YB = [2, 3]
                exp_hist = [[None] * 34, [None] * 34]
                cons_hist = [[None] * 34, [None] * 34]
                cons_hist[0][0], cons_hist[0][1] = pt_carry[0], pt_carry[1]
                cons_hist[1][0], cons_hist[1][1] = pt_carry[2], pt_carry[3]
                t_s_last = [None, None]
                t_y_last = [None, None]
                t_d_last = [None, None]
                for kc in range(32):
                    for st2 in range(2):
                        sbnk = SBKs[st2][kc % 2]
                        pti = 2 * st2 + kc % 2
                        ybnk = YB[st2]
                        t_s = mm(G[sbnk][:, 0:512], kp[:, kc * 128:(kc + 1) * 128],
                                 QB[st2], True, True,
                                 deps=[t_kl, t_ql[st2], gfree[sbnk],
                                       exp_hist[st2][kc]])
                        t_ex = sop(lambda e, sb=sbnk, pi=pti: e.activation(
                            PT[pi], G[sb][:, 0:512], AF.Exp, scale=ISQD),
                            [t_s, cons_hist[st2][kc]])
                        ydep = [t_ex]
                        if kc == 0:
                            ydep += [gfree[ybnk], t_vk_ev[-1], t_vk_ev[-2]]
                        t_y = mm(G[ybnk][:, 0:512], vk[:, kc * 128:(kc + 1) * 128],
                                 PT[pti], kc == 0, kc == 31, deps=ydep)
                        ddep = [t_ex]
                        if kc == 0:
                            ddep += [pdfree[st2], t_onesb]
                        t_d = mm(PD[st2][0:1, 0:512], ones_b[:], PT[pti],
                                 kc == 0, kc == 31, deps=ddep)
                        exp_hist[st2][kc + 2] = t_ex
                        cons_hist[st2][kc + 2] = t_d
                        t_s_last[st2] = t_s
                        t_y_last[st2] = t_y
                        t_d_last[st2] = t_d
                gfree[0] = exp_hist[0][32]
                gfree[1] = exp_hist[0][33]
                gfree[4] = exp_hist[1][32]
                gfree[5] = exp_hist[1][33]
                pt_carry = [cons_hist[0][32], cons_hist[0][33],
                            cons_hist[1][32], cons_hist[1][33]]
                for st2 in range(2):
                    ybnk = YB[st2]
                    t_r = vop(lambda e, p=st2: e.reciprocal(den_r[:], PD[p][0:1, 0:512]),
                              [t_d_last[st2]] + denr_free)
                    pdfree[st2] = t_r
                    t_b0 = dma("gpsimd", d_den[:], den_r[:], deps=[t_r], sem="d_den")
                    t_b1 = dma("gpsimd", den_b[:], d_den[:].to_broadcast([128, 512]),
                               deps=[t_b0] + denb_free, sem="d_den")
                    denr_free = [t_b0]
                    t_ye = vop(lambda e, h=h, qt=st2, b=ybnk: e.tensor_tensor(
                        V[:, h, qt * 512:(qt + 1) * 512], G[b][:, 0:512], den_b[:],
                        ALU.mult), [t_b1, t_y_last[st2]])
                    denb_free = [t_ye]
                    qb_free[st2] = [t_s_last[st2]]
                    t_ye_all.append(t_ye)
                    t_last_av = t_y_last[st2]
                    gfree[ybnk] = emit_ffn(hkv * 8 + g * 2 + st2, ybnk, t_ye)
            kp_free[bi] = [t_last_av]
            vk_free[bi] = [t_last_av]

        pwl_free = list(ffn_state["pw_free"])
        t_wt_ev = []
        for tc in range(OWN_CH):
            t_pl = dma("sync", pw_t[:], s_pw[tc],
                       deps=pwl_free + ffn_state["t_pw_st"], sem="d_pwl")
            for scu in range(8):
                bnk = scu % 4
                t_m = mm(G[bnk][:, 0:128], pw_t[:, scu * 128:(scu + 1) * 128],
                         eyeb[:], True, True, deps=[t_pl, t_eyeb, gfree[bnk],
                                                    t_last_av if tc == 0 else None])
                dst = WTap(scu, tc * 128, (tc + 1) * 128)
                srcp = G[bnk][:, 0:128]
                if scu % 2 == 0:
                    t_e = sop(lambda e, d=dst, s=srcp: e.copy(d, s), [t_m])
                else:
                    t_e = vop(lambda e, d=dst, s=srcp: e.tensor_copy(d, s), [t_m])
                gfree[bnk] = t_e
                t_wt_ev.append(t_e)
                t_lastm = t_m
            pwl_free = [t_lastm]

        avp_free = [[t_last_av], [t_last_av]]
        vdp_free = [[], []]
        ao_free = [t_last_av]
        t_t1m = [None] * 4
        for hc in range(32):
            bi = hc % 2
            avp = A[:, 2 + bi, 0:1024]
            t_avl = dma("sync", avp, s_av[hc], deps=avp_free[bi] + [t_av_rdy], sem=f"d_pan{bi}")
            vdp = wvd_b[:, bi, :]
            t_vdl = dma("sync", vdp, s_wvd[hc], deps=vdp_free[bi] + [t_wvd_rdy], sem=f"d_vdp{bi}")
            t_am2 = [None, None]
            for th in range(2):
                bnk = 4 + th
                t_last = None
                for scu in range(8):
                    dep = ([t_avl, gfree[bnk]] + t_wt_ev) if scu == 0 else []
                    t_last = mm(G[bnk][:, 0:512], avp[:, scu * 128:(scu + 1) * 128],
                                WTap(scu, th * 512, (th + 1) * 512),
                                scu == 0, scu == 7, deps=dep)
                t_am2[th] = t_last
            t_ae = sop(lambda e: e.copy(AOT[:, 0:512], G[4][:, 0:512]),
                       [t_am2[0]] + ao_free)
            t_ae2 = sop(lambda e: e.copy(AOT[:, 512:1024], G[5][:, 0:512]),
                        [t_am2[1]])
            gfree[4] = t_ae
            gfree[5] = t_ae2
            avp_free[bi] = [t_am2[1]]
            for rc in range(2):
                for th in range(2):
                    bnk = 2 * rc + th
                    dep = [t_vdl, t_ae, t_ae2]
                    if hc == 0:
                        dep.append(gfree[bnk])
                    t_t1m[bnk] = mm(G[bnk][:, 0:512],
                                    vdp[:, rc * 128:(rc + 1) * 128],
                                    AOT[:, th * 512:(th + 1) * 512],
                                    hc == 0, hc == 31, deps=dep)
            vdp_free[bi] = [t_t1m[3]]
            ao_free = [t_t1m[3]]
        t_t1ev = []
        for rc in range(2):
            for th in range(2):
                bnk = 2 * rc + th
                t_e = sop(lambda e, rc=rc, th=th, b=bnk: e.copy(
                    t1[:, rc, th * 512:(th + 1) * 512], G[b][:, 0:512]),
                    [t_t1m[bnk]])
                gfree[bnk] = t_e
                t_t1ev.append(t_e)

        wo_free = [[t_t1ev[-1]], [t_t1ev[-1]]]
        wvu_free = list(t_ye_all[-1:])
        bo1_free = []
        xp_free = [[], []]
        os_free = [[], []]
        t_out_st = []
        for osl in range(8):
            obi = osl % 2
            t_wol = []
            for r in range(4):
                t_wol.append(dma("sync", A[:, 4 * obi + r, :],
                                 s_wo[osl][:, r * 4096:(r + 1) * 4096],
                                 deps=wo_free[obi] + [t_wo_rdy]
                                 + (t_ye_all if osl < 2 else []), sem=f"d_wol{obi}"))
            t_wvl = []
            for rc in range(2):
                t_wvl.append(dma("sync", wvu_t[:, rc, :],
                                 s_wvu[osl, rc], deps=[t_wvu_rdy] + wvu_free,
                                 sem="d_wvl"))
            t_bol = dma("sync", bo1_t[:], s_bo[0:1, osl * 512:(osl + 1) * 512],
                        deps=[t_bos, prev_junk] + bo1_free, sem="d_bol")
            last_mm = None
            for tc in range(OWN_CH):
                bnk = tc % 4
                t_l = None
                for qh in range(HQ):
                    dep = []
                    if qh == 0:
                        dep = [gfree[bnk], t_ye_all[-1]] + t_wol
                    t_l = mm(G[bnk][:, 0:512],
                             V[:, qh, tc * 128:(tc + 1) * 128],
                             A[:, 4 * obi + qh // 8, (qh % 8) * 512:(qh % 8 + 1) * 512],
                             qh == 0, False, deps=dep)
                for rc in range(2):
                    t_l = mm(G[bnk][:, 0:512], t1[:, rc, tc * 128:(tc + 1) * 128],
                             wvu_t[:, rc, :], False, False,
                             deps=t_wvl + (t_t1ev if tc == 0 else []))
                t_l = mm(G[bnk][:, 0:512], ones1b[:], bo1_t[:], False, True,
                         deps=[t_ones1b, t_bol])
                xi = tc % 2
                t_xl = dma("sync", xpxs[:, xi, :],
                           s_xpx[tc][:, osl * 512:(osl + 1) * 512],
                           deps=xp_free[xi] + t_xpx_st, sem=f"d_xpl{xi}")
                t_ev = vop(lambda e, b=bnk, xi=xi: e.tensor_tensor(
                    ostg[:, xi, :], G[b][:, 0:512], xpxs[:, xi, :], ALU.add),
                    [t_l, t_xl] + os_free[xi])
                gfree[bnk] = t_ev
                xp_free[xi] = [t_ev]
                t_st = dma("gpsimd", y_out[tc][:, osl * 512:(osl + 1) * 512],
                           ostg[:, xi, :], deps=[t_ev], sem=f"d_ost{xi}")
                os_free[xi] = [t_st]
                t_out_st.append(t_st)
                last_mm = t_l
            wo_free[obi] = [last_mm]
            wvu_free = [last_mm]
            bo1_free = [last_mm]

        if DBG:
            t_dd = []
            def dump_via(dst_ap, src_ap, deps, prevtok):
                n_p = src_ap.shape[0]
                t_c = vop(lambda e, s=src_ap, n=n_p: e.tensor_copy(ostg[0:n, 0, 0:s.shape[-1]], s),
                          deps + prevtok)
                t_d = dma("gpsimd", dst_ap, ostg[0:n_p, 0, 0:src_ap.shape[-1]], deps=[t_c], sem="d_dump")
                return t_d
            prevt = list(os_free[0] + os_free[1])
            t_d1 = dump_via(dbg_gate[:], gate[:], t_gate, prevt); prevt = [t_d1]
            t_d2 = dump_via(dbg_sgt[:], sgt[:], t_sgt, prevt); prevt = [t_d2]
            t_dd += [t_d1, t_d2]
            t_dd.append(dma("gpsimd", dbg_lv[:], lv_all[:], deps=t_lv, sem="d_dump"))
            t_dd.append(dma("gpsimd", dbg_gs[:], gs_one[:], deps=[t_gi], sem="d_dump"))
            t_dd.append(dma("gpsimd", dbg_nm[:], negm[:], deps=[], sem="d_dump"))
            t_dd.append(dma("gpsimd", dbg_rs[:], rstd[:], deps=[], sem="d_dump"))
            for rc in range(2):
                for th in range(2):
                    t_dx = dump_via(dbg_t1[:, rc, th * 512:(th + 1) * 512],
                                    t1[:, rc, th * 512:(th + 1) * 512], t_t1ev, prevt)
                    prevt = [t_dx]; t_dd.append(t_dx)
            for th in range(2):
                t_dx = dump_via(dbg_q8[:, th * 512:(th + 1) * 512],
                                q8[:, th * 512:(th + 1) * 512], t_q8_ev, prevt)
                prevt = [t_dx]; t_dd.append(t_dx)
            t_dd.append(dma("gpsimd", dbg_wsum[:], wsum[:], deps=[], sem="d_dump"))
            for th in range(2):
                t_dx = dump_via(dbg_yt[:, th * 512:(th + 1) * 512],
                                V[:, 0, th * 512:(th + 1) * 512], t_ye_all, prevt)
                prevt = [t_dx]; t_dd.append(t_dx)
            for hh in range(HKV):
                t_dd.append(dma("gpsimd", dbg_k[hh], s_k[hh], deps=t_k_st, sem="d_dump"))
                t_dd.append(dma("gpsimd", dbg_v[hh], s_vd[hh], deps=t_v_st, sem="d_dump"))
            for qh in range(HQ):
                t_dd.append(dma("gpsimd", dbg_q[qh], s_q[qh], deps=t_q_st, sem="d_dump"))
            for cc in range(OWN_CH):
                t_dd.append(dma("gpsimd", dbg_xpx[cc], s_xpx[cc], deps=t_xpx_st, sem="d_dump"))
            t_out_st = t_out_st + t_dd
        sch.op("gpsimd", lambda e: e.memset(ones1b[:], 1.0), deps=t_out_st)

        sems = {s: es_.enter_context(nc.semaphore(s)) for s in sorted(sch.semnames)}
        block = es_.enter_context(nc.Block())
        sch.emit(block, sems)

    return nc


def kernel(**inp):
    x = np.asarray(inp["L0_attn_x"], dtype=np.float32)
    Wq = np.asarray(inp["L0_attn_Wq"], dtype=np.float32)
    Wk = np.asarray(inp["L0_attn_Wk"], dtype=np.float32)
    Wv = np.asarray(inp["L0_attn_Wv"], dtype=np.float32)
    Wo = np.asarray(inp["L0_attn_Wo"], dtype=np.float32)
    bo = np.asarray(inp["L0_attn_bo"], dtype=np.float32)
    fWq = np.asarray(inp["L0_ffn_Wq"], dtype=np.float32)
    fWr = np.asarray(inp["L0_ffn_Wr"], dtype=np.float32)
    fak = np.asarray(inp["L0_ffn_aux_keys"], dtype=np.float32)
    fav = np.asarray(inp["L0_ffn_aux_values"], dtype=np.float32)
    fWvd = np.asarray(inp["L0_ffn_Wvd"], dtype=np.float32)
    fWvu = np.asarray(inp["L0_ffn_Wvu"], dtype=np.float32)
    fWu1 = np.asarray(inp["L0_ffn_Wu1"], dtype=np.float32)
    fbu1 = np.asarray(inp["L0_ffn_bu1"], dtype=np.float32)
    fWu2 = np.asarray(inp["L0_ffn_Wu2"], dtype=np.float32)
    fbu2 = np.asarray(inp["L0_ffn_bu2"], dtype=np.float32)
    gw1 = np.asarray(inp["L0_ffn_gate_w1"], dtype=np.float32)
    gbi = np.asarray(inp["L0_ffn_gate_bias"], dtype=np.float32)
    flr = np.asarray(inp["L0_ffn_log_rel"], dtype=np.float32)

    C = np.ascontiguousarray
    wq_h = C(Wq.reshape(32, 128, 32, 128).transpose(0, 3, 2, 1).reshape(HQ, 128, E))
    wk_h = C(Wk.reshape(8, 128, 32, 128).transpose(0, 3, 2, 1).reshape(HKV, 128, E))
    wv_h = C(Wv.reshape(8, 128, 32, 128).transpose(0, 3, 2, 1).reshape(HKV, 128, E))
    wo_h = C(Wo.reshape(8, 512, 32, 128).transpose(0, 3, 2, 1).reshape(8, 128, 32 * 512))
    wu1_h = C(fWu1.reshape(8, 128, 32, 128).transpose(0, 3, 2, 1).reshape(8, 128, E))
    av_h = C(fav.reshape(8, 128, 32, 128).transpose(2, 1, 0, 3).reshape(32, 128, 8 * 128))
    wvd_h = C(fWvd.T.reshape(32, 128, VB))
    wvu_h = C(fWvu.reshape(8, 512, 2, 128).transpose(0, 2, 3, 1))
    wqf_h = C(fWq.reshape(BOT, 32, 128).transpose(2, 1, 0).reshape(128, 32 * BOT))
    wr_h = C(fWr.T)
    ak_h = C(fak.T)
    wu2_h = C(fWu2.reshape(8, 128).T)
    bu1_h = C(fbu1.reshape(8, 128).T)
    bu2_h = fbu2.reshape(1, 1)
    bo_h = bo.reshape(1, E)
    lr_h = flr.reshape(1, NSLOT)
    sc_h = np.array([[float(gw1), float(gbi)]], dtype=np.float32)
    eye_h = np.eye(128, dtype=np.float32)

    in_maps = []
    for c in range(8):
        b, q4 = c // 4, c % 4
        xo = np.roll(x[b], -q4 * 1024, axis=0)
        in_maps.append({
            "x_own": C(xo.reshape(NCH, 128, E)),
            "x_oth": C(x[1 - b].reshape(NCH, 128, E)),
            "wq_f": wq_h, "wk_f": wk_h, "wv_f": wv_h, "wo_f": wo_h,
            "wu1_f": wu1_h, "av_f": av_h, "wvd_f": wvd_h, "wvu_f": wvu_h,
            "wqf_f": wqf_h, "wr_f": wr_h, "ak_f": ak_h, "wu2_f": wu2_h,
            "bu1_f": bu1_h, "bu2_f": bu2_h, "bo_f": bo_h, "lr_f": lr_h,
            "sc_f": sc_h, "eye_f": eye_h,
        })

    nc = build_nc()
    res = run_bass_kernel_spmd(nc, in_maps, core_ids=list(range(8)))
    global _LAST_RES
    _LAST_RES = res
    out = np.empty((B, T, E), dtype=np.float32)
    for c in range(8):
        b, q4 = c // 4, c % 4
        out[b, q4 * 1024:(q4 + 1) * 1024] = res.results[c]["y"].reshape(1024, E)
    return out




# version 07
# speedup vs baseline: 1.4412x; 1.4412x over previous
import contextlib
import math
import numpy as np

import concourse.bass as bass
import concourse.mybir as mybir
from concourse.bass_utils import run_bass_kernel_spmd

F32 = mybir.dt.float32
BF16 = mybir.dt.bfloat16
AF = mybir.ActivationFunctionType
ALU = mybir.AluOpType
AX = mybir.AxisListType

B, T, E = 2, 4096, 4096
HKV, HQ, D = 8, 32, 128
NSLOT, BOT, RDIM, TOPK, VB = 1024, 64, 48, 8, 256
EPS = 1e-5
TAU_LOW, TAU_HIGH, EARLY_EXIT = 0.5, 2.0, 0.05
NCH, NSL, OWN_CH = 32, 8, 8
BIGC = 1.0e30
HBIG = 5.0e29
ISQD = 1.0 / math.sqrt(D)

ENGINES = ("sync", "scalar", "vector", "tensor", "gpsimd")
_LAST_RES = None


class Sched:
    def __init__(self):
        self.q = {e: [] for e in ENGINES}
        self.cnt = {}
        self.waited = {e: {} for e in ENGINES}
        self.semnames = set(e + s for e in ENGINES for s in ("_c", "_d"))

    def op(self, eng, fn, deps=(), dma=False, sem=None):
        sname = sem if sem else (eng + ("_d" if dma else "_c"))
        self.semnames.add(sname)
        waits = {}
        for tok in deps:
            if tok is None:
                continue
            s, v = tok
            if v <= self.waited[eng].get(s, 0):
                continue
            waits[s] = max(waits.get(s, 0), v)
        for s, v in waits.items():
            self.waited[eng][s] = v
        amt = 16 if dma else 1
        self.cnt[sname] = self.cnt.get(sname, 0) + amt
        tok = (sname, self.cnt[sname])
        self.q[eng].append((sorted(waits.items()), fn, sname, amt))
        return tok

    def emit(self, block, sems):
        def mk(engname):
            def _(eng):
                for waits, fn, sname, amt in self.q[engname]:
                    for s, v in waits:
                        eng.wait_ge(sems[s], v)
                    fn(eng).then_inc(sems[sname], amt)
            return _
        for e in ENGINES:
            if self.q[e]:
                getattr(block, e)(mk(e))


DBG = False
ABL = set()


def build_nc():
    nc = bass.Bass("TRN2", target_bir_lowering=False)

    def di(name, shape):
        return nc.dram_tensor(name, shape, F32, kind="ExternalInput")

    x_own = di("x_own", [NCH, 128, E])
    x_oth = di("x_oth", [NCH, 128, E])
    wq_f = di("wq_f", [HQ, 128, E])
    wk_f = di("wk_f", [HKV, 128, E])
    wv_f = di("wv_f", [HKV, 128, E])
    wo_f = di("wo_f", [8, 128, 32 * 512])
    wu1_f = di("wu1_f", [8, 128, E])
    av_f = di("av_f", [32, 128, 8 * 128])
    wvd_f = di("wvd_f", [32, 128, VB])
    wvu_f = di("wvu_f", [8, 2, 128, 512])
    wqf_f = di("wqf_f", [128, 32 * BOT])
    wr_f = di("wr_f", [BOT, RDIM])
    ak_f = di("ak_f", [BOT, NSLOT])
    wu2_f = di("wu2_f", [128, 8])
    bu1_f = di("bu1_f", [128, 8])
    bu2_f = di("bu2_f", [1, 1])
    bo_f = di("bo_f", [1, E])
    lr_f = di("lr_f", [1, NSLOT])
    sc_f = di("sc_f", [1, 2])
    eye_f = di("eye_f", [128, 128])

    y_out = nc.dram_tensor("y", [OWN_CH, 128, E], F32, kind="ExternalOutput")
    if DBG:
        dbg_gate = nc.dram_tensor("dbg_gate", [128, OWN_CH], F32, kind="ExternalOutput")
        dbg_sgt = nc.dram_tensor("dbg_sgt", [128, OWN_CH], F32, kind="ExternalOutput")
        dbg_lv = nc.dram_tensor("dbg_lv", [128, 64], F32, kind="ExternalOutput")
        dbg_gs = nc.dram_tensor("dbg_gs", [1, 4], F32, kind="ExternalOutput")
        dbg_nm = nc.dram_tensor("dbg_nm", [128, 32], F32, kind="ExternalOutput")
        dbg_rs = nc.dram_tensor("dbg_rs", [128, 32], F32, kind="ExternalOutput")
        dbg_t1 = nc.dram_tensor("dbg_t1", [128, 2, 1024], F32, kind="ExternalOutput")
        dbg_q8 = nc.dram_tensor("dbg_q8", [65, 1024], F32, kind="ExternalOutput")
        dbg_wsum = nc.dram_tensor("dbg_wsum", [128, 2], F32, kind="ExternalOutput")
        dbg_yt = nc.dram_tensor("dbg_yt", [128, 1024], F32, kind="ExternalOutput")
        dbg_k = nc.dram_tensor("dbg_k", [HKV, 128, T], BF16, kind="ExternalOutput")
        dbg_v = nc.dram_tensor("dbg_v", [HKV, 128, T], BF16, kind="ExternalOutput")
        dbg_q = nc.dram_tensor("dbg_q", [HQ, 128, 1024], BF16, kind="ExternalOutput")
        dbg_xpx = nc.dram_tensor("dbg_xpx", [OWN_CH, 128, E], F32, kind="ExternalOutput")

    def dscr(name, shape, dt=BF16):
        return nc.dram_tensor(name, shape, dt, kind="Internal")

    s_wq = dscr("s_wq", [HQ, 128, E])
    s_wk = dscr("s_wk", [HKV, 128, E])
    s_wv = dscr("s_wv", [HKV, 128, E])
    s_wo = dscr("s_wo", [8, 128, 32 * 512])
    s_wu1 = dscr("s_wu1", [8, 128, E])
    s_av = dscr("s_av", [32, 128, 8 * 128])
    s_wvd = dscr("s_wvd", [32, 128, VB])
    s_wvu = dscr("s_wvu", [8, 2, 128, 512])
    s_bo = dscr("s_bo", [1, E])
    s_q = dscr("s_q", [HQ, 128, 1024])
    s_k = dscr("s_k", [HKV, 128, T])
    s_vd = dscr("s_vd", [HKV, 128, T])
    s_xpx = dscr("s_xpx", [OWN_CH, 128, E], F32)
    s_pw = dscr("s_pw", [OWN_CH, 128, 1024], BF16)
    d_den = dscr("d_den", [1, 512], F32)
    d_bc = dscr("d_bc", [1, 1], F32)

    es_ = contextlib.ExitStack()
    with es_:
        def sb(name, shape, dt=F32):
            return es_.enter_context(nc.sbuf_tensor(name, shape, dt))

        A = sb("A", [128, 8, E], BF16)
        V = sb("V", [128, 32, 1024], BF16)
        BGF = sb("BGF", [128, 5, 2048], F32)

        Vflat = V[:].rearrange("p a b -> p (a b)")
        XNT = Vflat[:, 0:16384].rearrange("p (a b) -> p a b", a=32, b=512)
        H1 = Vflat[:, 16384:24576].rearrange("p (a b) -> p a b", a=8, b=1024)

        def WTap(sc, c0, c1):
            reg = 6 if sc < 4 else 7
            base = (sc % 4) * 1024
            return A[:, reg, base + c0:base + c1]

        XB = [BGF[:, 0:2, :].rearrange("p a b -> p (a b)"),
              BGF[:, 2:4, :].rearrange("p a b -> p (a b)")]
        CAST0 = BGF[:, 4, 0:1024]
        CAST1 = BGF[:, 4, 1024:2048]
        SRT = BGF[:, 0, 0:1024]
        SRW = BGF[:, 0, 1024:2048]
        ACH = BGF[:, 1, 0:1024]
        TBF = BGF[:, 1, 1024:2048]

        Af = A[:].rearrange("p a b -> p (a b)")
        XCB = A[:, 4, :]
        AOT = A[:, 4, 1024:2048]
        WQF = A[:, 5, 1024:3072]
        STG_K = A[:, 5, 0:512]
        STG_V = A[:, 6, 0:512]
        STG_Q = A[:, 6, 512:1024]
        JUNK = A[:, 7, :]
        QB = [A[:, 6, 0:512], A[:, 6, 512:1024]]
        PT = [A[:, 6, 1024 + 512 * i:1536 + 512 * i] for i in range(6)]
        WVU = A[:, 6, 2560:3584].rearrange("p (a b) -> p a b", a=2, b=512)
        BO1 = A[0:1, 7, 0:512]

        eyeb = sb("eyeb", [128, 128], BF16)
        ones_f = sb("ones_f", [128, 1])
        ones_b = sb("ones_b", [128, 1], BF16)
        ones1b = sb("ones1b", [1, 128], BF16)
        stat = sb("stat", [128, 16])
        negm = sb("negm", [128, NCH])
        rstd = sb("rstd", [128, NCH])
        a01 = sb("a01", [128, 2])
        lv_all = sb("lv_all", [128, 2 * NCH])
        gate = sb("gate", [128, OWN_CH])
        sgt = sb("sgt", [128, OWN_CH])
        gtmp = sb("gtmp", [128, 4])
        ginv_b = sb("ginv_b", [128, 1])
        gw1_b = sb("gw1_b", [128, 1])
        gb_b = sb("gb_b", [128, 1])
        gs_row = sb("gs_row", [1, 64])
        gs_one = sb("gs_one", [1, 4])
        dmat = sb("dmat", [128, 128], BF16)
        q8 = sb("q8", [BOT + 1, 1024], BF16)
        rq_a = sb("rq_a", [65, 1024], BF16)
        rk_a = sb("rk_a", [65, NSLOT], BF16)
        ak_a = sb("ak_a", [BOT + 1, NSLOT], BF16)
        wr_b = sb("wr_b", [BOT, RDIM], BF16)
        wu2_b = sb("wu2_b", [128, 8], BF16)
        bu1_s = sb("bu1_s", [128, 8])
        bu2_s = sb("bu2_s", [1, 1])
        t1 = sb("t1", [128, 2, 1024], BF16)
        wvd_b = sb("wvd_b", [128, 2, VB], BF16)
        wvu_t = sb("wvu_t", [128, 2, 512], BF16)
        bo1_t = sb("bo1_t", [1, 512], BF16)
        pw_t = sb("pw_t", [128, 1024], BF16)
        sgrow = sb("sgrow", [1, 1024], BF16)
        wsum = sb("wsum", [128, 2])
        winv = sb("winv", [128, 2])
        den_r = sb("den_r", [1, 512])
        den_b = sb("den_b", [128, 512])
        ostg = sb("ostg", [128, 2, 512])
        xpxs = sb("xpxs", [128, 2, 512])

        G = [es_.enter_context(nc.psum_tensor(f"g{i}", [128, 512], F32))
             for i in range(6)]
        PD = [es_.enter_context(nc.psum_tensor("pd0", [1, 512], F32)),
              es_.enter_context(nc.psum_tensor("pd1", [1, 512], F32))]

        sch = Sched()

        def dma(queue, out, in_, deps=(), sem=None):
            return sch.op(queue, lambda e, o=out, i=in_: e.dma_start(out=o, in_=i),
                          deps=deps, dma=True, sem=sem)

        def mm(out, lhsT, rhs, start, stop, deps=()):
            return sch.op("tensor",
                          lambda e, o=out, l=lhsT, r=rhs, s0=start, s1=stop:
                          e.matmul(o, l, r, start=s0, stop=s1), deps=deps)

        def vop(fn, deps=()):
            return sch.op("vector", fn, deps=deps)

        def sop(fn, deps=()):
            return sch.op("scalar", fn, deps=deps)

        gfree = [None] * 6
        pdfree = [None, None]

        t_eyf = dma("sync", BGF[:, 4, 0:128], eye_f[:], sem="d_eye")
        t_eyeb = vop(lambda e: e.tensor_copy(eyeb[:], BGF[:, 4, 0:128]), [t_eyf])
        t_onesf = vop(lambda e: e.memset(ones_f[:], 1.0))
        t_onesb = vop(lambda e: e.memset(ones_b[:], 1.0))
        t_ones1b = vop(lambda e: e.memset(ones1b[:], 1.0))
        t_gw1 = dma("gpsimd", gw1_b[:], sc_f[0:1, 0:1].to_broadcast([128, 1]), sem="d_gw1")
        t_gb = dma("gpsimd", gb_b[:], sc_f[0:1, 1:2].to_broadcast([128, 1]), sem="d_gb")
        t_bu1 = dma("sync", bu1_s[:], bu1_f[:], sem="d_bu1")
        t_bu2 = dma("sync", bu2_s[:], bu2_f[:], sem="d_bu2")

        t_wrf = dma("sync", BGF[0:BOT, 4, 128:176], wr_f[:], deps=[], sem="d_wr")
        t_wrb = vop(lambda e: e.tensor_copy(wr_b[:], BGF[0:BOT, 4, 128:176]), [t_wrf])
        t_akf = dma("sync", CAST1[0:BOT, :], ak_f[:], deps=[], sem="d_ak")
        t_akb = vop(lambda e: e.tensor_copy(ak_a[0:BOT, :], CAST1[0:BOT, :]), [t_akf])
        t_lrf = dma("sync", BGF[0:1, 4, 176:1200], lr_f[:], deps=[], sem="d_lr")
        t_lrb = vop(lambda e: e.tensor_copy(ak_a[BOT:BOT + 1, :], BGF[0:1, 4, 176:1200]),
                    [t_lrf])
        t_rqz = vop(lambda e: e.memset(rq_a[32:64, :], 0.0))
        t_rkz = vop(lambda e: e.memset(rk_a[32:64, :], 0.0))
        t_rklr = vop(lambda e: e.tensor_copy(rk_a[64:65, :],
                                             ak_a[BOT:BOT + 1, :]), [t_lrb, t_rkz])
        t_wu2f = dma("sync", BGF[:, 4, 1200:1208], wu2_f[:], deps=[], sem="d_wu2")
        t_wu2b = vop(lambda e: e.tensor_copy(wu2_b[:], BGF[:, 4, 1200:1208]), [t_wu2f])
        t_bof = dma("sync", XB[0][0:1, :], bo_f[:], deps=[], sem="d_boin")
        t_bob = vop(lambda e: e.tensor_copy(JUNK[0:1, 0:E], XB[0][0:1, :]), [t_bof])
        t_bos = dma("gpsimd", s_bo[:], JUNK[0:1, 0:E], deps=[t_bob], sem="d_bos")
        t_wqff = dma("sync", XB[1][:, 0:2048], wqf_f[:], deps=[], sem="d_wqf")
        t_wqfb = vop(lambda e: e.tensor_copy(WQF, XB[1][:, 0:2048]), [t_wqff])

        cast_jobs = []
        for j in range(HKV):
            cast_jobs.append((wk_f[j], s_wk[j], "wk", j))
        for j in range(HKV):
            cast_jobs.append((wv_f[j], s_wv[j], "wv", j))
        for j in range(HQ):
            cast_jobs.append((wq_f[j], s_wq[j], "wq", j))
        for j in range(8):
            cast_jobs.append((wu1_f[j], s_wu1[j], "wu1", j))
        for j in range(32):
            cast_jobs.append((av_f[j], s_av[j], "av", j))
        for j in range(32):
            cast_jobs.append((wvd_f[j], s_wvd[j], "wvd", j))
        for j in range(8):
            for rc in range(2):
                cast_jobs.append((wvu_f[j, rc], s_wvu[j, rc], "wvu", j))
        for j in range(8):
            cast_jobs.append((wo_f[j], s_wo[j], "wo", j))

        CASTB = [CAST0, CAST1]
        COUTB = [A[:, 0, 0:1024], A[:, 0, 1024:2048]]
        cin_free = [[t_eyeb, t_wrb], [t_akb, t_wu2b]]
        cout_free = [[], []]
        rdy = {}
        ci = 0
        for src, dst, tag, j in cast_jobs:
            total = src.shape[-2] * src.shape[-1] if len(src.shape) > 2 else None
            flat_src = src.rearrange("p f -> p f") if len(src.shape) == 2 else src
            free = src.shape[-1]
            ntile = (free + 1023) // 1024
            t_last = None
            for tt in range(ntile):
                f0 = tt * 1024
                f1 = min(free, f0 + 1024)
                w = f1 - f0
                bi = ci % 2
                ci += 1
                t_ld = dma("sync", CASTB[bi][:, 0:w], src[:, f0:f1], deps=cin_free[bi], sem=f"d_ci{bi}")
                if bi == 0:
                    t_c = sop(lambda e, o=COUTB[bi][:, 0:w], i=CASTB[bi][:, 0:w]:
                              e.copy(o, i), deps=[t_ld] + cout_free[bi])
                else:
                    t_c = vop(lambda e, o=COUTB[bi][:, 0:w], i=CASTB[bi][:, 0:w]:
                              e.tensor_copy(o, i), deps=[t_ld] + cout_free[bi])
                cin_free[bi] = [t_c]
                t_st = dma("gpsimd", dst[:, f0:f1], COUTB[bi][:, 0:w], deps=[t_c], sem=f"d_cst{bi}")
                cout_free[bi] = [t_st]
                t_last = t_st
            rdy[(tag, j)] = t_last
        t_wk_rdy = rdy[("wk", HKV - 1)]
        t_wv_rdy = rdy[("wv", HKV - 1)]
        t_wq_rdy = rdy[("wq", HQ - 1)]
        t_wu1_rdy = rdy[("wu1", 7)]
        t_av_rdy = rdy[("av", 31)]
        t_wvd_rdy = rdy[("wvd", 31)]
        t_wvu_rdy = rdy[("wvu", 7)]
        t_wo_rdy = rdy[("wo", 7)]


        t_lv = [None] * 64
        stat_free = [[], []]

        def chunk_stats(xb, t_ld, cc_lv, own_cc, prev_junk):
            si = (cc_lv % 2) * 8
            cs = [stat[:, si + k:si + k + 1] for k in range(8)]
            fr = stat_free[cc_lv % 2]
            t_sum = vop(lambda e: e.reduce_sum(cs[0], xb, axis=AX.X), [t_ld] + fr)
            t_sq2 = sop(lambda e: e.activation(JUNK, xb, AF.Square, accum_out=cs[2]),
                        [t_ld, prev_junk] + fr)
            t_mean = vop(lambda e: e.tensor_scalar(cs[1], cs[0], 1.0 / E, None,
                                                   ALU.mult), [t_sum])
            tn = tr = None
            if own_cc is not None:
                tn = vop(lambda e, cc=own_cc: e.tensor_scalar(
                    negm[:, cc:cc + 1], cs[1], -1.0, None, ALU.mult), [t_mean])
            t_msq = vop(lambda e: e.tensor_scalar(cs[3], cs[1], cs[1], None, ALU.mult),
                        [t_mean])
            t_var = vop(lambda e: e.scalar_tensor_tensor(
                cs[3], cs[2], 1.0 / E, cs[3], ALU.mult, ALU.subtract), [t_sq2, t_msq])
            t_veps = vop(lambda e: e.tensor_scalar(cs[4], cs[3], EPS, None, ALU.add),
                         [t_var])
            if own_cc is not None:
                t_sqr = sop(lambda e: e.activation(cs[5], cs[4], AF.Sqrt), [t_veps])
                tr = vop(lambda e, cc=own_cc: e.reciprocal(rstd[:, cc:cc + 1], cs[5]),
                         [t_sqr])
            if "lv" in ABL:
                tl = vop(lambda e, cc=cc_lv: e.memset(lv_all[:, cc:cc + 1], 0.5),
                         [])
                t_lv[cc_lv] = tl
                stat_free[cc_lv % 2] = [tl, t_veps] + ([tr] if tr else [])
                return tn, tr, tl, t_sq2
            t_rv = vop(lambda e: e.reciprocal(cs[6], cs[4]), [t_veps])
            t_vx = vop(lambda e: e.tensor_tensor(cs[7], cs[3], cs[6], ALU.mult),
                       [t_var, t_rv])
            tl = sop(lambda e, cc=cc_lv: e.activation(
                lv_all[:, cc:cc + 1], cs[7], AF.Ln, bias=1.0), [t_vx])
            t_lv[cc_lv] = tl
            stat_free[cc_lv % 2] = [tl] + ([tr] if tr else [])
            return tn, tr, tl, t_sq2

        xb_free = [[t_bob], [t_wqfb]]
        xcb_free = []
        dmat_free = []
        prev_junk = t_bos
        t_xpx_st = []
        t_q_st, t_k_st, t_v_st = [], [], []
        t_h1_ev = []
        t_q8_ev = []
        pan_free = [[], []]
        stgK_free, stgV_free, stgQ_free = [], [], []

        for sl in range(NSL):
            t_xnt_ev = []
            for j in range(4):
                cc = 4 * sl + j
                bi = cc % 2
                xb = XB[bi]
                t_ld = dma("sync", xb, x_own[cc],
                           deps=xb_free[bi], sem=f"d_xb{bi}")
                tn, tr, tl, prev_junk = chunk_stats(xb, t_ld, cc, cc, prev_junk)
                new_xb_free = [tl, prev_junk]
                t_xcb = vop(lambda e, xb=xb, cc=cc: e.tensor_scalar(
                    XCB, xb, negm[:, cc:cc + 1], None, ALU.add),
                    [t_ld, tn] + xcb_free)
                new_xb_free.append(t_xcb)
                if cc < OWN_CH and "xpx" not in ABL:
                    t_a1 = vop(lambda e, cc=cc: e.tensor_scalar(
                        a01[:, 1:2], rstd[:, cc:cc + 1], 1.0, None, ALU.add), [tr])
                    t_a0 = vop(lambda e, cc=cc: e.tensor_tensor(
                        a01[:, 0:1], negm[:, cc:cc + 1], rstd[:, cc:cc + 1],
                        ALU.mult), [tn, tr])
                    t_p1 = vop(lambda e, xb=xb: e.tensor_scalar(
                        xb, xb, a01[:, 1:2], None, ALU.mult),
                        [t_a1, t_xcb, tl, prev_junk])
                    t_p2 = vop(lambda e, xb=xb: e.tensor_scalar(
                        xb, xb, a01[:, 0:1], None, ALU.add), [t_a0, t_p1])
                    t_xst = dma("gpsimd", s_xpx[cc], xb,
                                deps=[t_p2], sem="d_xpxs")
                    t_xpx_st.append(t_xst)
                    new_xb_free = [t_xst]
                xb_free[bi] = new_xb_free
                t_dm = vop(lambda e, cc=cc: e.tensor_scalar(
                    dmat[:], eyeb[:], rstd[:, cc:cc + 1], None, ALU.mult),
                    [t_eyeb, tr] + dmat_free)
                tms = []
                for ec in range(32):
                    bnk = ec % 4
                    t_m = mm(G[bnk][:, 0:128], XCB[:, ec * 128:(ec + 1) * 128],
                             dmat[:], True, True, deps=[t_xcb, t_dm, gfree[bnk]])
                    dst = XNT[:, ec, j * 128:(j + 1) * 128]
                    src = G[bnk][:, 0:128]
                    if ec % 2 == 0:
                        t_e = sop(lambda e, d=dst, s=src: e.copy(d, s), [t_m])
                    else:
                        t_e = vop(lambda e, d=dst, s=src: e.tensor_copy(d, s), [t_m])
                    gfree[bnk] = t_e
                    tms.append(t_m)
                    t_xnt_ev.append(t_e)
                xcb_free = [tms[-1]]
                dmat_free = [tms[-1]]

            def proj32(pan, bnk, first_deps, out_ap=None):
                t_last = None
                oap = G[bnk][:, 0:512] if out_ap is None else out_ap
                for ec in range(32):
                    dep = first_deps if ec == 0 else []
                    t_last = mm(oap, pan[:, ec * 128:(ec + 1) * 128],
                                XNT[:, ec, :], ec == 0, ec == 31, deps=dep)
                return t_last

            for hc in range(HKV):
                bi = hc % 2
                pan = A[:, 2 + bi, :]
                t_pl = dma("sync", pan, s_wk[hc], deps=pan_free[bi] + [t_wk_rdy], sem=f"d_pan{bi}")
                bnk = 4 + (hc % 2)
                t_mmE = proj32(pan, bnk, [t_pl, gfree[bnk]] + t_xnt_ev)
                t_ev = sop(lambda e, b=bnk: e.copy(STG_K, G[b][:, 0:512]),
                           [t_mmE] + stgK_free)
                t_st = dma("gpsimd", s_k[hc][:, sl * 512:(sl + 1) * 512], STG_K,
                           deps=[t_ev], sem="d_kst")
                gfree[bnk] = t_ev
                stgK_free = [t_st]
                pan_free[bi] = [t_mmE]
                t_k_st.append(t_st)

            for hc in ([] if "vproj" in ABL else range(HKV)):
                bi = hc % 2
                pan = A[:, 2 + bi, :]
                t_pl = dma("sync", pan, s_wv[hc], deps=pan_free[bi] + [t_wv_rdy], sem=f"d_pan{bi}")
                bnk = 4 + (hc % 2)
                t_mmE = proj32(pan, bnk, [t_pl, gfree[bnk]] + t_xnt_ev)
                t_ev = sop(lambda e, b=bnk: e.copy(STG_V, G[b][:, 0:512]),
                           [t_mmE] + stgV_free)
                t_st = dma("gpsimd", s_vd[hc][:, sl * 512:(sl + 1) * 512], STG_V,
                           deps=[t_ev], sem="d_vst")
                gfree[bnk] = t_ev
                stgV_free = [t_st]
                pan_free[bi] = [t_mmE]
                t_v_st.append(t_st)

            if sl < 2:
                for qh in range(HQ):
                    bi = qh % 2
                    pan = A[:, 2 + bi, :]
                    t_pl = dma("sync", pan, s_wq[qh], deps=pan_free[bi] + [t_wq_rdy], sem=f"d_pan{bi}")
                    bnk = 4 + (qh % 2)
                    t_mmE = proj32(pan, bnk, [t_pl, gfree[bnk]] + t_xnt_ev)
                    t_ev = sop(lambda e, b=bnk: e.copy(STG_Q, G[b][:, 0:512]),
                               [t_mmE] + stgQ_free)
                    t_st = dma("gpsimd", s_q[qh][:, sl * 512:(sl + 1) * 512], STG_Q,
                               deps=[t_ev], sem="d_qst")
                    gfree[bnk] = t_ev
                    stgQ_free = [t_st]
                    pan_free[bi] = [t_mmE]
                    t_q_st.append(t_st)
                for uc in ([] if "wu1" in ABL else range(8)):
                    bi = uc % 2
                    pan = A[:, 2 + bi, :]
                    t_pl = dma("sync", pan, s_wu1[uc], deps=pan_free[bi] + [t_wu1_rdy], sem=f"d_pan{bi}")
                    bnk = 4 + (uc % 2)
                    t_mmE = proj32(pan, bnk, [t_pl, gfree[bnk]] + t_xnt_ev)
                    t_ev = sop(lambda e, b=bnk, uc=uc, sl=sl: e.activation(
                        H1[:, uc, sl * 512:(sl + 1) * 512], G[b][:, 0:512],
                        AF.Gelu_apprx_tanh, bias=bu1_s[:, uc:uc + 1]),
                        [t_mmE, t_bu1])
                    gfree[bnk] = t_ev
                    pan_free[bi] = [t_mmE]
                    t_h1_ev.append(t_ev)
                if "query" in ABL:
                    t_ev = vop(lambda e, sl=sl: e.memset(
                        q8[0:BOT, sl * 512:(sl + 1) * 512], 0.01), [])
                else:
                    bnk = 4
                    t_mmq = None
                    for ec in range(32):
                        dep = ([t_wqfb, gfree[bnk]] + t_xnt_ev) if ec == 0 else []
                        t_mmq = mm(G[bnk][0:BOT, 0:512],
                                   WQF[:, ec * BOT:(ec + 1) * BOT],
                                   XNT[:, ec, :], ec == 0, ec == 31, deps=dep)
                    t_ev = vop(lambda e, sl=sl, b=bnk: e.tensor_scalar(
                        q8[0:BOT, sl * 512:(sl + 1) * 512], G[b][0:BOT, 0:512],
                        0.125, None, ALU.mult), [t_mmq])
                    gfree[bnk] = t_ev
                t_q8_ev.append(t_ev)

            for j2 in range(4):
                cc2 = 4 * sl + j2
                bi = cc2 % 2
                xb = XB[bi]
                t_ld = dma("sync", xb, x_oth[cc2],
                           deps=xb_free[bi], sem=f"d_xb{bi}")
                _, _, tl2, prev_junk = chunk_stats(xb, t_ld, NCH + cc2, None,
                                                   prev_junk)
                xb_free[bi] = [tl2, prev_junk]


        pass

        bnk = 5
        t_gmm = mm(G[bnk][0:1, 0:64], ones_f[:], lv_all[:, 0:64], True, True,
                   deps=[t_onesf, gfree[bnk]] + t_lv)
        t_gcp = vop(lambda e, b=bnk: e.tensor_copy(gs_row[:], G[b][0:1, 0:64]), [t_gmm])
        gfree[bnk] = t_gcp
        t_gred = vop(lambda e: e.reduce_sum(gs_one[0:1, 0:1], gs_row[:], axis=AX.X),
                     [t_gcp])
        t_gm = vop(lambda e: e.tensor_scalar(gs_one[0:1, 1:2], gs_one[0:1, 0:1],
                                             1.0 / (B * T), 1e-6, ALU.mult,
                                             op1=ALU.add), [t_gred])
        t_gi = vop(lambda e: e.reciprocal(gs_one[0:1, 2:3], gs_one[0:1, 1:2]), [t_gm])
        t_gb0 = dma("gpsimd", d_bc[:], gs_one[0:1, 2:3], deps=[t_gi], sem="d_gbc")
        t_gbc = dma("gpsimd", ginv_b[:], d_bc[:].to_broadcast([128, 1]), deps=[t_gb0], sem="d_gbc")

        t_sgr = []
        for th in range(2):
            bnk = 5
            t_mml = None
            for uc in range(8):
                dep = ([t_wu2b, gfree[bnk]] + t_h1_ev) if uc == 0 else []
                t_mml = mm(G[bnk][0:1, 0:512], wu2_b[:, uc:uc + 1],
                           H1[:, uc, th * 512:(th + 1) * 512], uc == 0, uc == 7,
                           deps=dep)
            t_sgv = sop(lambda e, th=th, b=bnk: e.activation(
                sgrow[0:1, th * 512:(th + 1) * 512], G[b][0:1, 0:512],
                AF.Sigmoid, bias=bu2_s[0:1, 0:1]), [t_mml, t_bu2])
            gfree[bnk] = t_sgv
            t_sgr.append(t_sgv)
        t_sgt = []
        for tc in range(OWN_CH):
            bnk = 5
            t_mt = mm(G[bnk][0:128, 0:1], sgrow[0:1, tc * 128:(tc + 1) * 128],
                      ones_b[0:1, 0:1], True, True,
                      deps=[t_onesb, gfree[bnk]] + t_sgr)
            t_e = vop(lambda e, tc=tc, b=bnk: e.tensor_copy(
                sgt[:, tc:tc + 1], G[b][0:128, 0:1]), [t_mt])
            gfree[bnk] = t_e
            t_sgt.append(t_e)

        t_gate = []
        gtmp_free = []
        for tc in range(OWN_CH):
            g0 = gtmp[:, 0:1]
            g1 = gtmp[:, 1:2]
            t_nv = vop(lambda e, tc=tc: e.tensor_scalar(
                g0, lv_all[:, tc:tc + 1], ginv_b[:, 0:1], 0.5, ALU.mult,
                op1=ALU.mult), [t_gbc, t_lv[tc]] + gtmp_free)
            t_u = vop(lambda e, tc=tc: e.scalar_tensor_tensor(
                g1, sgt[:, tc:tc + 1], 2.5, g0, ALU.mult, ALU.add),
                [t_nv, t_sgt[tc]])
            t_u2 = vop(lambda e: e.tensor_scalar(g1, g1, 5.0, 0.0, ALU.min,
                                                 op1=ALU.max), [t_u])
            t_un = vop(lambda e: e.tensor_scalar(
                g1, g1, -TAU_LOW, 1.0 / (TAU_HIGH - TAU_LOW), ALU.add,
                op1=ALU.mult), [t_u2])
            t_un2 = vop(lambda e: e.tensor_scalar(g1, g1, 1.0, 0.0, ALU.min,
                                                  op1=ALU.max), [t_un])
            t_gl = vop(lambda e: e.tensor_scalar(g1, g1, gw1_b[:, 0:1], None,
                                                 ALU.mult), [t_un2, t_gw1])
            t_gl2 = vop(lambda e: e.tensor_scalar(g1, g1, gb_b[:, 0:1], None,
                                                  ALU.add), [t_gl, t_gb])
            t_sig = sop(lambda e: e.activation(g0, g1, AF.Sigmoid), [t_gl2])
            t_msk = vop(lambda e: e.tensor_scalar(g1, g0, EARLY_EXIT, None,
                                                  ALU.is_ge), [t_sig])
            t_g = vop(lambda e, tc=tc: e.tensor_tensor(
                gate[:, tc:tc + 1], g0, g1, ALU.mult), [t_msk])
            gtmp_free = [t_g]
            t_gate.append(t_g)

        t_q8o = vop(lambda e: e.memset(q8[BOT:BOT + 1, :], 1.0))
        t_rq = []
        for th in range(2):
            bnk = 5
            t_m = mm(G[bnk][0:RDIM, 0:512], wr_b[:],
                     q8[0:BOT, th * 512:(th + 1) * 512], True, True,
                     deps=[t_wrb, gfree[bnk]] + t_q8_ev)
            t_e = vop(lambda e, th=th, b=bnk: e.tensor_scalar(
                rq_a[0:RDIM, th * 512:(th + 1) * 512], G[b][0:RDIM, 0:512],
                8.0 / math.sqrt(RDIM), None, ALU.mult), [t_m, t_rqz])
            gfree[bnk] = t_e
            t_rq.append(t_e)
        t_rqo = vop(lambda e: e.memset(rq_a[64:65, :], 1.0), [t_rqz])
        t_rk = [t_rklr]
        for th in range(2):
            bnk = 5
            t_m = mm(G[bnk][0:RDIM, 0:512], wr_b[:],
                     ak_a[0:BOT, th * 512:(th + 1) * 512], True, True,
                     deps=[t_wrb, t_akb, gfree[bnk]])
            t_e = sop(lambda e, th=th, b=bnk: e.copy(
                rk_a[0:RDIM, th * 512:(th + 1) * 512], G[b][0:RDIM, 0:512]),
                [t_m, t_rkz])
            gfree[bnk] = t_e
            t_rk.append(t_e)

        ffn_state = {
            "pw_free": [], "srt_free": list(xb_free[0]), "ach_free": list(xb_free[0]),
            "tbf_free": list(xb_free[0]), "srw_free": list(xb_free[0]),
            "t_sr": [], "t_ach": [], "t_w": None, "t_m8": None,
        }

        def emit_ffn(i, ybnk, t_slot):
            fs = ffn_state
            tc, st = i // 8, i % 8
            c_m = gtmp[:, 2:3]
            bankfree = t_slot
            if st == 0:
                fs["t_sr"] = []
                fs["t_ach"] = []
                t_m = mm(G[ybnk][:, 0:512], rq_a[:, tc * 128:(tc + 1) * 128],
                         rk_a[:, 0:512], True, True,
                         deps=[t_rqo, t_slot] + t_rq + t_rk)
                t_e = sop(lambda e, b=ybnk: e.copy(SRT[:, 0:512], G[b][:, 0:512]),
                          [t_m] + fs["srt_free"])
                fs["srt_free"] = []
                fs["t_sr"].append(t_e)
                bankfree = t_e
            elif st == 1:
                t_m = mm(G[ybnk][:, 0:512], rq_a[:, tc * 128:(tc + 1) * 128],
                         rk_a[:, 512:1024], True, True,
                         deps=[t_rqo, t_slot] + t_rq + t_rk)
                t_e = sop(lambda e, b=ybnk: e.copy(SRT[:, 512:1024], G[b][:, 0:512]),
                          [t_m])
                fs["t_sr"].append(t_e)
                fs["t_w"] = vop(lambda e: e.tensor_copy(SRW[:], SRT[:]),
                                fs["t_sr"] + fs["srw_free"])
                fs["srw_free"] = []
                bankfree = t_e
            elif st == 2:
                t_m = mm(G[ybnk][:, 0:512], q8[:, tc * 128:(tc + 1) * 128],
                         ak_a[:, 0:512], True, True,
                         deps=[t_q8o, t_akb, t_lrb, t_slot] + t_q8_ev)
                t_e = sop(lambda e, b=ybnk: e.copy(ACH[:, 0:512], G[b][:, 0:512]),
                          [t_m] + fs["ach_free"])
                fs["ach_free"] = []
                fs["t_ach"].append(t_e)
                bankfree = t_e
                t_mx = vop(lambda e: e.reduce_max(c_m, SRW[:], axis=AX.X),
                           [fs["t_w"]])
                t_tb = vop(lambda e: e.tensor_scalar(
                    TBF[:], SRW[:], c_m, BIGC, ALU.is_lt, op1=ALU.mult),
                    [t_mx] + fs["tbf_free"])
                fs["tbf_free"] = []
                fs["t_w"] = vop(lambda e: e.scalar_tensor_tensor(
                    SRW[:], TBF[:], HBIG, SRW[:], ALU.subtract, ALU.min), [t_tb])
            elif st == 3:
                t_m = mm(G[ybnk][:, 0:512], q8[:, tc * 128:(tc + 1) * 128],
                         ak_a[:, 512:1024], True, True,
                         deps=[t_q8o, t_akb, t_lrb, t_slot] + t_q8_ev)
                t_e = sop(lambda e, b=ybnk: e.copy(ACH[:, 512:1024], G[b][:, 0:512]),
                          [t_m])
                fs["t_ach"].append(t_e)
                bankfree = t_e
                for _ in range(2):
                    t_mx = vop(lambda e: e.reduce_max(c_m, SRW[:], axis=AX.X),
                               [fs["t_w"]])
                    t_tb = vop(lambda e: e.tensor_scalar(
                        TBF[:], SRW[:], c_m, BIGC, ALU.is_lt, op1=ALU.mult), [t_mx])
                    fs["t_w"] = vop(lambda e: e.scalar_tensor_tensor(
                        SRW[:], TBF[:], HBIG, SRW[:], ALU.subtract, ALU.min), [t_tb])
            elif st in (4, 5):
                for _ in range(2):
                    t_mx = vop(lambda e: e.reduce_max(c_m, SRW[:], axis=AX.X),
                               [fs["t_w"]])
                    t_tb = vop(lambda e: e.tensor_scalar(
                        TBF[:], SRW[:], c_m, BIGC, ALU.is_lt, op1=ALU.mult), [t_mx])
                    fs["t_w"] = vop(lambda e: e.scalar_tensor_tensor(
                        SRW[:], TBF[:], HBIG, SRW[:], ALU.subtract, ALU.min), [t_tb])
            elif st == 6:
                t_mx = vop(lambda e: e.reduce_max(c_m, SRW[:], axis=AX.X),
                           [fs["t_w"]])
                t_tb = vop(lambda e: e.tensor_scalar(
                    TBF[:], SRW[:], c_m, BIGC, ALU.is_lt, op1=ALU.mult), [t_mx])
                t_w = vop(lambda e: e.scalar_tensor_tensor(
                    SRW[:], TBF[:], HBIG, SRW[:], ALU.subtract, ALU.min), [t_tb])
                t_m8 = vop(lambda e: e.reduce_max(c_m, SRW[:], axis=AX.X), [t_w])
                fs["srw_free"] = [t_m8]
                t_tb2 = vop(lambda e: e.tensor_scalar(
                    TBF[:], SRT[:], c_m, BIGC, ALU.is_lt, op1=ALU.mult),
                    [t_m8] + fs["t_sr"])
                fs["srt_free"] = [t_tb2]
                t_akn = vop(lambda e: e.scalar_tensor_tensor(
                    ACH[:], TBF[:], HBIG, ACH[:], ALU.subtract, ALU.min),
                    [t_tb2] + fs["t_ach"])
                fs["tbf_free"] = [t_akn]
                t_am = vop(lambda e: e.reduce_max(gtmp[:, 3:4], ACH[:], axis=AX.X),
                           [t_akn])
                fs["t_nam"] = vop(lambda e: e.tensor_scalar(
                    gtmp[:, 2:3], gtmp[:, 3:4], -1.0, None, ALU.mult), [t_am])
                fs["t_akn"] = t_akn
            else:
                wi = tc % 2
                t_exp = sop(lambda e, wi=wi: e.activation(
                    pw_t[:], ACH[:], AF.Exp, bias=gtmp[:, 2:3],
                    accum_out=wsum[:, wi:wi + 1]),
                    [fs["t_nam"], fs["t_akn"]] + fs["pw_free"])
                fs["ach_free"] = [t_exp]
                t_wi = vop(lambda e, wi=wi: e.reciprocal(
                    winv[:, wi:wi + 1], wsum[:, wi:wi + 1]), [t_exp])
                t_sc2 = vop(lambda e, wi=wi, tc=tc: e.tensor_tensor(
                    winv[:, wi:wi + 1], winv[:, wi:wi + 1], gate[:, tc:tc + 1],
                    ALU.mult), [t_wi, t_gate[tc]])
                t_pws = vop(lambda e, wi=wi: e.tensor_scalar(
                    pw_t[:], pw_t[:], winv[:, wi:wi + 1], None, ALU.mult),
                    [t_sc2, t_exp])
                t_pst = dma("gpsimd", s_pw[tc], pw_t[:], deps=[t_pws], sem="d_pw")
                fs["pw_free"] = [t_pst]
                fs["t_pw_st"] = fs.get("t_pw_st", []) + [t_pst]
            return bankfree

        kp_free = [cout_free[0] + cout_free[1], []]
        vp_free = [list(pan_free[0]), list(pan_free[1])]
        vk_free = [[], [t_k_st[-1]]]
        qb_free = [[t_v_st[-1] if t_v_st else None], [t_q_st[-1] if t_q_st else None]]
        pt_carry = [None] * 6
        denb_free = []
        denr_free = []
        t_ye_all = []
        for hkv in range(HKV):
            bi = hkv % 2
            kp = A[:, 0 + bi, :]
            vp = A[:, 2 + bi, :]
            t_kl = dma("sync", kp, s_k[hkv], deps=kp_free[bi] + t_k_st, sem=f"d_kp{bi}")
            t_vl = dma("sync", vp, s_vd[hkv], deps=vp_free[bi] + t_v_st, sem=f"d_vp{bi}")
            vk = A[:, 4 + bi, :]
            t_vk_ev = []
            for kc in range(32):
                bnk = 2 if kc % 2 == 0 else 3
                t_m = mm(G[bnk][:, 0:128], vp[:, kc * 128:(kc + 1) * 128], eyeb[:],
                         True, True,
                         deps=[t_vl, t_eyeb, gfree[bnk]] + vk_free[bi])
                dst = vk[:, kc * 128:(kc + 1) * 128]
                src = G[bnk][:, 0:128]
                if kc % 2 == 0:
                    t_e = sop(lambda e, d=dst, s=src: e.copy(d, s), [t_m])
                else:
                    t_e = vop(lambda e, d=dst, s=src: e.tensor_copy(d, s), [t_m])
                gfree[bnk] = t_e
                t_vk_ev.append(t_e)
            vk_free[bi] = []
            vp_free[bi] = [t_vk_ev[-1]]
            t_last_av = None
            for g in range(4):
                h = hkv * 4 + g
                t_ql = [dma("sync", QB[qi], s_q[h][:, qi * 512:(qi + 1) * 512],
                            deps=qb_free[qi] + t_q_st, sem=f"d_qb{qi}")
                        for qi in range(2)]
                SBKs = [[0, 1], [4, 5]]
                YB = [2, 3]
                exp_hist = [[None] * 40, [None] * 40]
                cons_hist = [[None] * 40, [None] * 40]
                for st2 in range(2):
                    for k in range(3):
                        cons_hist[st2][k] = pt_carry[3 * st2 + k]
                t_s_last = [None, None]
                t_y_last = [None, None]
                t_d_last = [None, None]
                t_ex_all = [[None] * 32, [None] * 32]

                def emit_S(kc, st2):
                    sbnk = SBKs[st2][kc % 2]
                    pti = 3 * st2 + kc % 3
                    t_s = mm(G[sbnk][:, 0:512], kp[:, kc * 128:(kc + 1) * 128],
                             QB[st2], True, True,
                             deps=[t_kl, t_ql[st2], gfree[sbnk],
                                   exp_hist[st2][kc]])
                    t_ex = sop(lambda e, sb=sbnk, pi=pti: e.activation(
                        PT[pi], G[sb][:, 0:512], AF.Exp, scale=ISQD),
                        [t_s, cons_hist[st2][kc]])
                    exp_hist[st2][kc + 2] = t_ex
                    t_ex_all[st2][kc] = t_ex
                    t_s_last[st2] = t_s

                def emit_YD(kc, st2):
                    pti = 3 * st2 + kc % 3
                    ybnk = YB[st2]
                    t_ex = t_ex_all[st2][kc]
                    ydep = [t_ex]
                    if kc == 0:
                        ydep += [gfree[ybnk], t_vk_ev[-1], t_vk_ev[-2]]
                    t_y = mm(G[ybnk][:, 0:512], vk[:, kc * 128:(kc + 1) * 128],
                             PT[pti], kc == 0, kc == 31, deps=ydep)
                    ddep = [t_ex]
                    if kc == 0:
                        ddep += [pdfree[st2], t_onesb]
                    t_d = mm(PD[st2][0:1, 0:512], ones_b[:], PT[pti],
                             kc == 0, kc == 31, deps=ddep)
                    cons_hist[st2][kc + 3] = t_d
                    t_y_last[st2] = t_y
                    t_d_last[st2] = t_d

                for kc in range(32):
                    emit_S(kc, 0)
                    emit_S(kc, 1)
                    if kc >= 2:
                        emit_YD(kc - 2, 0)
                        emit_YD(kc - 2, 1)
                for kc in (30, 31):
                    emit_YD(kc, 0)
                    emit_YD(kc, 1)
                gfree[0] = exp_hist[0][32]
                gfree[1] = exp_hist[0][33]
                gfree[4] = exp_hist[1][32]
                gfree[5] = exp_hist[1][33]
                pt_carry = [None] * 6
                for st2 in range(2):
                    for s in range(3):
                        kc = [k for k in (29, 30, 31) if k % 3 == s][0]
                        pt_carry[3 * st2 + s] = cons_hist[st2][kc + 3]
                for st2 in range(2):
                    ybnk = YB[st2]
                    t_r = vop(lambda e, p=st2: e.reciprocal(den_r[:], PD[p][0:1, 0:512]),
                              [t_d_last[st2]] + denr_free)
                    pdfree[st2] = t_r
                    t_b0 = dma("gpsimd", d_den[:], den_r[:], deps=[t_r], sem="d_den")
                    t_b1 = dma("gpsimd", den_b[:], d_den[:].to_broadcast([128, 512]),
                               deps=[t_b0] + denb_free, sem="d_den")
                    denr_free = [t_b0]
                    t_ye = vop(lambda e, h=h, qt=st2, b=ybnk: e.tensor_tensor(
                        V[:, h, qt * 512:(qt + 1) * 512], G[b][:, 0:512], den_b[:],
                        ALU.mult), [t_b1, t_y_last[st2]])
                    denb_free = [t_ye]
                    qb_free[st2] = [t_s_last[st2]]
                    t_ye_all.append(t_ye)
                    t_last_av = t_y_last[st2]
                    gfree[ybnk] = emit_ffn(hkv * 8 + g * 2 + st2, ybnk, t_ye)
            kp_free[bi] = [t_last_av]
            vk_free[bi] = [t_last_av]

        pwl_free = list(ffn_state["pw_free"])
        t_wt_ev = []
        for tc in range(OWN_CH):
            t_pl = dma("sync", pw_t[:], s_pw[tc],
                       deps=pwl_free + ffn_state["t_pw_st"], sem="d_pwl")
            for scu in range(8):
                bnk = scu % 4
                t_m = mm(G[bnk][:, 0:128], pw_t[:, scu * 128:(scu + 1) * 128],
                         eyeb[:], True, True, deps=[t_pl, t_eyeb, gfree[bnk],
                                                    t_last_av if tc == 0 else None])
                dst = WTap(scu, tc * 128, (tc + 1) * 128)
                srcp = G[bnk][:, 0:128]
                if scu % 2 == 0:
                    t_e = sop(lambda e, d=dst, s=srcp: e.copy(d, s), [t_m])
                else:
                    t_e = vop(lambda e, d=dst, s=srcp: e.tensor_copy(d, s), [t_m])
                gfree[bnk] = t_e
                t_wt_ev.append(t_e)
                t_lastm = t_m
            pwl_free = [t_lastm]

        avp_free = [[t_last_av], [t_last_av]]
        vdp_free = [[], []]
        ao_free = [t_last_av]
        t_t1m = [None] * 4
        for hc in range(32):
            bi = hc % 2
            avp = A[:, 2 + bi, 0:1024]
            t_avl = dma("sync", avp, s_av[hc], deps=avp_free[bi] + [t_av_rdy], sem=f"d_pan{bi}")
            vdp = wvd_b[:, bi, :]
            t_vdl = dma("sync", vdp, s_wvd[hc], deps=vdp_free[bi] + [t_wvd_rdy], sem=f"d_vdp{bi}")
            t_am2 = [None, None]
            for th in range(2):
                bnk = 4 + th
                t_last = None
                for scu in range(8):
                    dep = ([t_avl, gfree[bnk]] + t_wt_ev) if scu == 0 else []
                    t_last = mm(G[bnk][:, 0:512], avp[:, scu * 128:(scu + 1) * 128],
                                WTap(scu, th * 512, (th + 1) * 512),
                                scu == 0, scu == 7, deps=dep)
                t_am2[th] = t_last
            t_ae = sop(lambda e: e.copy(AOT[:, 0:512], G[4][:, 0:512]),
                       [t_am2[0]] + ao_free)
            t_ae2 = sop(lambda e: e.copy(AOT[:, 512:1024], G[5][:, 0:512]),
                        [t_am2[1]])
            gfree[4] = t_ae
            gfree[5] = t_ae2
            avp_free[bi] = [t_am2[1]]
            for rc in range(2):
                for th in range(2):
                    bnk = 2 * rc + th
                    dep = [t_vdl, t_ae, t_ae2]
                    if hc == 0:
                        dep.append(gfree[bnk])
                    t_t1m[bnk] = mm(G[bnk][:, 0:512],
                                    vdp[:, rc * 128:(rc + 1) * 128],
                                    AOT[:, th * 512:(th + 1) * 512],
                                    hc == 0, hc == 31, deps=dep)
            vdp_free[bi] = [t_t1m[3]]
            ao_free = [t_t1m[3]]
        t_t1ev = []
        for rc in range(2):
            for th in range(2):
                bnk = 2 * rc + th
                t_e = sop(lambda e, rc=rc, th=th, b=bnk: e.copy(
                    t1[:, rc, th * 512:(th + 1) * 512], G[b][:, 0:512]),
                    [t_t1m[bnk]])
                gfree[bnk] = t_e
                t_t1ev.append(t_e)

        wo_free = [[t_t1ev[-1]], [t_t1ev[-1]]]
        wvu_free = list(t_ye_all[-1:])
        bo1_free = []
        xp_free = [[], []]
        os_free = [[], []]
        t_out_st = []
        for osl in range(8):
            obi = osl % 2
            t_wol = []
            for r in range(4):
                t_wol.append(dma("sync", A[:, 4 * obi + r, :],
                                 s_wo[osl][:, r * 4096:(r + 1) * 4096],
                                 deps=wo_free[obi] + [t_wo_rdy]
                                 + (t_ye_all if osl < 2 else []), sem=f"d_wol{obi}"))
            t_wvl = []
            for rc in range(2):
                t_wvl.append(dma("sync", wvu_t[:, rc, :],
                                 s_wvu[osl, rc], deps=[t_wvu_rdy] + wvu_free,
                                 sem="d_wvl"))
            t_bol = dma("sync", bo1_t[:], s_bo[0:1, osl * 512:(osl + 1) * 512],
                        deps=[t_bos, prev_junk] + bo1_free, sem="d_bol")
            last_mm = None
            for tc in range(OWN_CH):
                bnk = tc % 4
                t_l = None
                for qh in range(HQ):
                    dep = []
                    if qh == 0:
                        dep = [gfree[bnk], t_ye_all[-1]] + t_wol
                    t_l = mm(G[bnk][:, 0:512],
                             V[:, qh, tc * 128:(tc + 1) * 128],
                             A[:, 4 * obi + qh // 8, (qh % 8) * 512:(qh % 8 + 1) * 512],
                             qh == 0, False, deps=dep)
                for rc in range(2):
                    t_l = mm(G[bnk][:, 0:512], t1[:, rc, tc * 128:(tc + 1) * 128],
                             wvu_t[:, rc, :], False, False,
                             deps=t_wvl + (t_t1ev if tc == 0 else []))
                t_l = mm(G[bnk][:, 0:512], ones1b[:], bo1_t[:], False, True,
                         deps=[t_ones1b, t_bol])
                xi = tc % 2
                t_xl = dma("sync", xpxs[:, xi, :],
                           s_xpx[tc][:, osl * 512:(osl + 1) * 512],
                           deps=xp_free[xi] + t_xpx_st, sem=f"d_xpl{xi}")
                t_ev = vop(lambda e, b=bnk, xi=xi: e.tensor_tensor(
                    ostg[:, xi, :], G[b][:, 0:512], xpxs[:, xi, :], ALU.add),
                    [t_l, t_xl] + os_free[xi])
                gfree[bnk] = t_ev
                xp_free[xi] = [t_ev]
                t_st = dma("gpsimd", y_out[tc][:, osl * 512:(osl + 1) * 512],
                           ostg[:, xi, :], deps=[t_ev], sem=f"d_ost{xi}")
                os_free[xi] = [t_st]
                t_out_st.append(t_st)
                last_mm = t_l
            wo_free[obi] = [last_mm]
            wvu_free = [last_mm]
            bo1_free = [last_mm]

        if DBG:
            t_dd = []
            def dump_via(dst_ap, src_ap, deps, prevtok):
                n_p = src_ap.shape[0]
                t_c = vop(lambda e, s=src_ap, n=n_p: e.tensor_copy(ostg[0:n, 0, 0:s.shape[-1]], s),
                          deps + prevtok)
                t_d = dma("gpsimd", dst_ap, ostg[0:n_p, 0, 0:src_ap.shape[-1]], deps=[t_c], sem="d_dump")
                return t_d
            prevt = list(os_free[0] + os_free[1])
            t_d1 = dump_via(dbg_gate[:], gate[:], t_gate, prevt); prevt = [t_d1]
            t_d2 = dump_via(dbg_sgt[:], sgt[:], t_sgt, prevt); prevt = [t_d2]
            t_dd += [t_d1, t_d2]
            t_dd.append(dma("gpsimd", dbg_lv[:], lv_all[:], deps=t_lv, sem="d_dump"))
            t_dd.append(dma("gpsimd", dbg_gs[:], gs_one[:], deps=[t_gi], sem="d_dump"))
            t_dd.append(dma("gpsimd", dbg_nm[:], negm[:], deps=[], sem="d_dump"))
            t_dd.append(dma("gpsimd", dbg_rs[:], rstd[:], deps=[], sem="d_dump"))
            for rc in range(2):
                for th in range(2):
                    t_dx = dump_via(dbg_t1[:, rc, th * 512:(th + 1) * 512],
                                    t1[:, rc, th * 512:(th + 1) * 512], t_t1ev, prevt)
                    prevt = [t_dx]; t_dd.append(t_dx)
            for th in range(2):
                t_dx = dump_via(dbg_q8[:, th * 512:(th + 1) * 512],
                                q8[:, th * 512:(th + 1) * 512], t_q8_ev, prevt)
                prevt = [t_dx]; t_dd.append(t_dx)
            t_dd.append(dma("gpsimd", dbg_wsum[:], wsum[:], deps=[], sem="d_dump"))
            for th in range(2):
                t_dx = dump_via(dbg_yt[:, th * 512:(th + 1) * 512],
                                V[:, 0, th * 512:(th + 1) * 512], t_ye_all, prevt)
                prevt = [t_dx]; t_dd.append(t_dx)
            for hh in range(HKV):
                t_dd.append(dma("gpsimd", dbg_k[hh], s_k[hh], deps=t_k_st, sem="d_dump"))
                t_dd.append(dma("gpsimd", dbg_v[hh], s_vd[hh], deps=t_v_st, sem="d_dump"))
            for qh in range(HQ):
                t_dd.append(dma("gpsimd", dbg_q[qh], s_q[qh], deps=t_q_st, sem="d_dump"))
            for cc in range(OWN_CH):
                t_dd.append(dma("gpsimd", dbg_xpx[cc], s_xpx[cc], deps=t_xpx_st, sem="d_dump"))
            t_out_st = t_out_st + t_dd
        sch.op("gpsimd", lambda e: e.memset(ones1b[:], 1.0), deps=t_out_st)

        sems = {s: es_.enter_context(nc.semaphore(s)) for s in sorted(sch.semnames)}
        block = es_.enter_context(nc.Block())
        sch.emit(block, sems)

    return nc


def kernel(**inp):
    x = np.asarray(inp["L0_attn_x"], dtype=np.float32)
    Wq = np.asarray(inp["L0_attn_Wq"], dtype=np.float32)
    Wk = np.asarray(inp["L0_attn_Wk"], dtype=np.float32)
    Wv = np.asarray(inp["L0_attn_Wv"], dtype=np.float32)
    Wo = np.asarray(inp["L0_attn_Wo"], dtype=np.float32)
    bo = np.asarray(inp["L0_attn_bo"], dtype=np.float32)
    fWq = np.asarray(inp["L0_ffn_Wq"], dtype=np.float32)
    fWr = np.asarray(inp["L0_ffn_Wr"], dtype=np.float32)
    fak = np.asarray(inp["L0_ffn_aux_keys"], dtype=np.float32)
    fav = np.asarray(inp["L0_ffn_aux_values"], dtype=np.float32)
    fWvd = np.asarray(inp["L0_ffn_Wvd"], dtype=np.float32)
    fWvu = np.asarray(inp["L0_ffn_Wvu"], dtype=np.float32)
    fWu1 = np.asarray(inp["L0_ffn_Wu1"], dtype=np.float32)
    fbu1 = np.asarray(inp["L0_ffn_bu1"], dtype=np.float32)
    fWu2 = np.asarray(inp["L0_ffn_Wu2"], dtype=np.float32)
    fbu2 = np.asarray(inp["L0_ffn_bu2"], dtype=np.float32)
    gw1 = np.asarray(inp["L0_ffn_gate_w1"], dtype=np.float32)
    gbi = np.asarray(inp["L0_ffn_gate_bias"], dtype=np.float32)
    flr = np.asarray(inp["L0_ffn_log_rel"], dtype=np.float32)

    C = np.ascontiguousarray
    wq_h = C(Wq.reshape(32, 128, 32, 128).transpose(0, 3, 2, 1).reshape(HQ, 128, E))
    wk_h = C(Wk.reshape(8, 128, 32, 128).transpose(0, 3, 2, 1).reshape(HKV, 128, E))
    wv_h = C(Wv.reshape(8, 128, 32, 128).transpose(0, 3, 2, 1).reshape(HKV, 128, E))
    wo_h = C(Wo.reshape(8, 512, 32, 128).transpose(0, 3, 2, 1).reshape(8, 128, 32 * 512))
    wu1_h = C(fWu1.reshape(8, 128, 32, 128).transpose(0, 3, 2, 1).reshape(8, 128, E))
    av_h = C(fav.reshape(8, 128, 32, 128).transpose(2, 1, 0, 3).reshape(32, 128, 8 * 128))
    wvd_h = C(fWvd.T.reshape(32, 128, VB))
    wvu_h = C(fWvu.reshape(8, 512, 2, 128).transpose(0, 2, 3, 1))
    wqf_h = C(fWq.reshape(BOT, 32, 128).transpose(2, 1, 0).reshape(128, 32 * BOT))
    wr_h = C(fWr.T)
    ak_h = C(fak.T)
    wu2_h = C(fWu2.reshape(8, 128).T)
    bu1_h = C(fbu1.reshape(8, 128).T)
    bu2_h = fbu2.reshape(1, 1)
    bo_h = bo.reshape(1, E)
    lr_h = flr.reshape(1, NSLOT)
    sc_h = np.array([[float(gw1), float(gbi)]], dtype=np.float32)
    eye_h = np.eye(128, dtype=np.float32)

    in_maps = []
    for c in range(8):
        b, q4 = c // 4, c % 4
        xo = np.roll(x[b], -q4 * 1024, axis=0)
        in_maps.append({
            "x_own": C(xo.reshape(NCH, 128, E)),
            "x_oth": C(x[1 - b].reshape(NCH, 128, E)),
            "wq_f": wq_h, "wk_f": wk_h, "wv_f": wv_h, "wo_f": wo_h,
            "wu1_f": wu1_h, "av_f": av_h, "wvd_f": wvd_h, "wvu_f": wvu_h,
            "wqf_f": wqf_h, "wr_f": wr_h, "ak_f": ak_h, "wu2_f": wu2_h,
            "bu1_f": bu1_h, "bu2_f": bu2_h, "bo_f": bo_h, "lr_f": lr_h,
            "sc_f": sc_h, "eye_f": eye_h,
        })

    nc = build_nc()
    res = run_bass_kernel_spmd(nc, in_maps, core_ids=list(range(8)))
    global _LAST_RES
    _LAST_RES = res
    out = np.empty((B, T, E), dtype=np.float32)
    for c in range(8):
        b, q4 = c // 4, c % 4
        out[b, q4 * 1024:(q4 + 1) * 1024] = res.results[c]["y"].reshape(1024, E)
    return out




# version 09 (final)
# speedup vs baseline: 1.4683x; 1.0188x over previous
import contextlib
import math
import numpy as np

import concourse.bass as bass
import concourse.mybir as mybir
from concourse.bass_utils import run_bass_kernel_spmd

F32 = mybir.dt.float32
BF16 = mybir.dt.bfloat16
AF = mybir.ActivationFunctionType
ALU = mybir.AluOpType
AX = mybir.AxisListType

B, T, E = 2, 4096, 4096
HKV, HQ, D = 8, 32, 128
NSLOT, BOT, RDIM, TOPK, VB = 1024, 64, 48, 8, 256
EPS = 1e-5
TAU_LOW, TAU_HIGH, EARLY_EXIT = 0.5, 2.0, 0.05
NCH, NSL, OWN_CH = 32, 8, 8
BIGC = 1.0e30
HBIG = 5.0e29
ISQD = 1.0 / math.sqrt(D)

ENGINES = ("sync", "scalar", "vector", "tensor", "gpsimd")
_LAST_RES = None


class Sched:
    def __init__(self):
        self.q = {e: [] for e in ENGINES}
        self.cnt = {}
        self.waited = {e: {} for e in ENGINES}
        self.semnames = set(e + s for e in ENGINES for s in ("_c", "_d"))

    def op(self, eng, fn, deps=(), dma=False, sem=None):
        sname = sem if sem else (eng + ("_d" if dma else "_c"))
        self.semnames.add(sname)
        waits = {}
        for tok in deps:
            if tok is None:
                continue
            s, v = tok
            if v <= self.waited[eng].get(s, 0):
                continue
            waits[s] = max(waits.get(s, 0), v)
        for s, v in waits.items():
            self.waited[eng][s] = v
        amt = 16 if dma else 1
        self.cnt[sname] = self.cnt.get(sname, 0) + amt
        tok = (sname, self.cnt[sname])
        self.q[eng].append((sorted(waits.items()), fn, sname, amt))
        return tok

    def emit(self, block, sems):
        def mk(engname):
            def _(eng):
                for waits, fn, sname, amt in self.q[engname]:
                    for s, v in waits:
                        eng.wait_ge(sems[s], v)
                    fn(eng).then_inc(sems[sname], amt)
            return _
        for e in ENGINES:
            if self.q[e]:
                getattr(block, e)(mk(e))


DBG = False
ABL = set()


def build_nc():
    nc = bass.Bass("TRN2", target_bir_lowering=False)

    def di(name, shape):
        return nc.dram_tensor(name, shape, F32, kind="ExternalInput")

    x_own = di("x_own", [NCH, 128, E])
    x_oth = di("x_oth", [NCH, 128, E])
    wq_f = di("wq_f", [HQ, 128, E])
    wk_f = di("wk_f", [HKV, 128, E])
    wv_f = di("wv_f", [HKV, 128, E])
    wo_f = di("wo_f", [8, 128, 32 * 512])
    wu1_f = di("wu1_f", [8, 128, E])
    av_f = di("av_f", [32, 128, 8 * 128])
    wvd_f = di("wvd_f", [32, 128, VB])
    wvu_f = di("wvu_f", [8, 2, 128, 512])
    wqf_f = di("wqf_f", [128, 32 * BOT])
    wr_f = di("wr_f", [BOT, RDIM])
    ak_f = di("ak_f", [BOT, NSLOT])
    wu2_f = di("wu2_f", [128, 8])
    bu1_f = di("bu1_f", [128, 8])
    bu2_f = di("bu2_f", [1, 1])
    bo_f = di("bo_f", [1, E])
    lr_f = di("lr_f", [1, NSLOT])
    sc_f = di("sc_f", [1, 2])
    eye_f = di("eye_f", [128, 128])

    y_out = nc.dram_tensor("y", [OWN_CH, 128, E], F32, kind="ExternalOutput")
    if DBG:
        dbg_gate = nc.dram_tensor("dbg_gate", [128, OWN_CH], F32, kind="ExternalOutput")
        dbg_sgt = nc.dram_tensor("dbg_sgt", [128, OWN_CH], F32, kind="ExternalOutput")
        dbg_lv = nc.dram_tensor("dbg_lv", [128, 64], F32, kind="ExternalOutput")
        dbg_gs = nc.dram_tensor("dbg_gs", [1, 4], F32, kind="ExternalOutput")
        dbg_nm = nc.dram_tensor("dbg_nm", [128, 32], F32, kind="ExternalOutput")
        dbg_rs = nc.dram_tensor("dbg_rs", [128, 32], F32, kind="ExternalOutput")
        dbg_t1 = nc.dram_tensor("dbg_t1", [128, 2, 1024], F32, kind="ExternalOutput")
        dbg_q8 = nc.dram_tensor("dbg_q8", [65, 1024], F32, kind="ExternalOutput")
        dbg_wsum = nc.dram_tensor("dbg_wsum", [128, 2], F32, kind="ExternalOutput")
        dbg_yt = nc.dram_tensor("dbg_yt", [128, 1024], F32, kind="ExternalOutput")
        dbg_k = nc.dram_tensor("dbg_k", [HKV, 128, T], BF16, kind="ExternalOutput")
        dbg_v = nc.dram_tensor("dbg_v", [HKV, 128, T], BF16, kind="ExternalOutput")
        dbg_q = nc.dram_tensor("dbg_q", [HQ, 128, 1024], BF16, kind="ExternalOutput")
        dbg_xpx = nc.dram_tensor("dbg_xpx", [OWN_CH, 128, E], F32, kind="ExternalOutput")

    def dscr(name, shape, dt=BF16):
        return nc.dram_tensor(name, shape, dt, kind="Internal")

    s_wq = dscr("s_wq", [HQ, 128, E])
    s_wk = dscr("s_wk", [HKV, 128, E])
    s_wv = dscr("s_wv", [HKV, 128, E])
    s_wo = dscr("s_wo", [8, 128, 32 * 512])
    s_wu1 = dscr("s_wu1", [8, 128, E])
    s_av = dscr("s_av", [32, 128, 8 * 128])
    s_wvd = dscr("s_wvd", [32, 128, VB])
    s_wvu = dscr("s_wvu", [8, 2, 128, 512])
    s_bo = dscr("s_bo", [1, E])
    s_q = dscr("s_q", [HQ, 128, 1024])
    s_k = dscr("s_k", [HKV, 128, T])
    s_vd = dscr("s_vd", [HKV, 128, T])
    s_xpx = dscr("s_xpx", [OWN_CH, 128, E], F32)
    s_pw = dscr("s_pw", [OWN_CH, 128, 1024], BF16)
    d_den = dscr("d_den", [1, 512], F32)
    d_bc = dscr("d_bc", [1, 1], F32)

    es_ = contextlib.ExitStack()
    with es_:
        def sb(name, shape, dt=F32):
            return es_.enter_context(nc.sbuf_tensor(name, shape, dt))

        A = sb("A", [128, 8, E], BF16)
        V = sb("V", [128, 32, 1024], BF16)
        BGF = sb("BGF", [128, 5, 2048], F32)

        Vflat = V[:].rearrange("p a b -> p (a b)")
        XNT = Vflat[:, 0:16384].rearrange("p (a b) -> p a b", a=32, b=512)
        H1 = Vflat[:, 16384:24576].rearrange("p (a b) -> p a b", a=8, b=1024)

        def WTap(sc, c0, c1):
            reg = 6 if sc < 4 else 7
            base = (sc % 4) * 1024
            return A[:, reg, base + c0:base + c1]

        XB = [BGF[:, 0:2, :].rearrange("p a b -> p (a b)"),
              BGF[:, 2:4, :].rearrange("p a b -> p (a b)")]
        CAST0 = BGF[:, 4, 0:1024]
        CAST1 = BGF[:, 4, 1024:2048]
        SRT = BGF[:, 0, 0:1024]
        SRW = BGF[:, 0, 1024:2048]
        ACH = BGF[:, 1, 0:1024]
        TBF = BGF[:, 1, 1024:2048]

        Af = A[:].rearrange("p a b -> p (a b)")
        XCB = A[:, 4, :]
        AOT = A[:, 4, 1024:2048]
        WQF = A[:, 5, 1024:3072]
        STG_K = A[:, 5, 0:512]
        STG_V = A[:, 6, 0:512]
        STG_Q = A[:, 6, 512:1024]
        JUNK = A[:, 7, :]
        QB = [A[:, 6, 0:512], A[:, 6, 512:1024]]
        PT = [A[:, 6, 1024 + 512 * i:1536 + 512 * i] for i in range(6)]
        WVU = A[:, 6, 2560:3584].rearrange("p (a b) -> p a b", a=2, b=512)
        BO1 = A[0:1, 7, 0:512]

        eyeb = sb("eyeb", [128, 128], BF16)
        ones_f = sb("ones_f", [128, 1])
        ones_b = sb("ones_b", [128, 1], BF16)
        ones1b = sb("ones1b", [1, 128], BF16)
        stat = sb("stat", [128, 16])
        negm = sb("negm", [128, NCH])
        rstd = sb("rstd", [128, NCH])
        a01 = sb("a01", [128, 2])
        lv_all = sb("lv_all", [128, 2 * NCH])
        gate = sb("gate", [128, OWN_CH])
        sgt = sb("sgt", [128, OWN_CH])
        gtmp = sb("gtmp", [128, 4])
        ginv_b = sb("ginv_b", [128, 1])
        gw1_b = sb("gw1_b", [128, 1])
        gb_b = sb("gb_b", [128, 1])
        gs_row = sb("gs_row", [1, 64])
        gs_one = sb("gs_one", [1, 4])
        dmat = sb("dmat", [128, 128], BF16)
        q8 = sb("q8", [BOT + 1, 1024], BF16)
        rq_a = sb("rq_a", [65, 1024], BF16)
        rk_a = sb("rk_a", [65, NSLOT], BF16)
        ak_a = sb("ak_a", [BOT + 1, NSLOT], BF16)
        wr_b = sb("wr_b", [BOT, RDIM], BF16)
        wu2_b = sb("wu2_b", [128, 8], BF16)
        bu1_s = sb("bu1_s", [128, 8])
        bu2_s = sb("bu2_s", [1, 1])
        t1 = sb("t1", [128, 2, 1024], BF16)
        wvd_b = sb("wvd_b", [128, 2, VB], BF16)
        wvu_t = sb("wvu_t", [128, 2, 512], BF16)
        bo1_t = sb("bo1_t", [1, 512], BF16)
        pw_t = sb("pw_t", [128, 1024], BF16)
        sgrow = sb("sgrow", [1, 1024], BF16)
        wsum = sb("wsum", [128, 2])
        winv = sb("winv", [128, 2])
        den_r = sb("den_r", [1, 512])
        den_b = sb("den_b", [128, 512])
        ostg = sb("ostg", [128, 4, 512])
        xpxs = sb("xpxs", [128, 2, 512])

        G = [es_.enter_context(nc.psum_tensor(f"g{i}", [128, 512], F32))
             for i in range(6)]
        PD = [es_.enter_context(nc.psum_tensor("pd0", [1, 512], F32)),
              es_.enter_context(nc.psum_tensor("pd1", [1, 512], F32))]

        sch = Sched()

        def dma(queue, out, in_, deps=(), sem=None):
            return sch.op(queue, lambda e, o=out, i=in_: e.dma_start(out=o, in_=i),
                          deps=deps, dma=True, sem=sem)

        def mm(out, lhsT, rhs, start, stop, deps=()):
            return sch.op("tensor",
                          lambda e, o=out, l=lhsT, r=rhs, s0=start, s1=stop:
                          e.matmul(o, l, r, start=s0, stop=s1), deps=deps)

        def vop(fn, deps=()):
            return sch.op("vector", fn, deps=deps)

        def sop(fn, deps=()):
            return sch.op("scalar", fn, deps=deps)

        gfree = [None] * 6
        pdfree = [None, None]

        t_eyf = dma("sync", BGF[:, 4, 0:128], eye_f[:], sem="d_eye")
        t_eyeb = vop(lambda e: e.tensor_copy(eyeb[:], BGF[:, 4, 0:128]), [t_eyf])
        t_onesf = vop(lambda e: e.memset(ones_f[:], 1.0))
        t_onesb = vop(lambda e: e.memset(ones_b[:], 1.0))
        t_ones1b = vop(lambda e: e.memset(ones1b[:], 1.0))
        t_gw1 = dma("gpsimd", gw1_b[:], sc_f[0:1, 0:1].to_broadcast([128, 1]), sem="d_gw1")
        t_gb = dma("gpsimd", gb_b[:], sc_f[0:1, 1:2].to_broadcast([128, 1]), sem="d_gb")
        t_bu1 = dma("sync", bu1_s[:], bu1_f[:], sem="d_bu1")
        t_bu2 = dma("sync", bu2_s[:], bu2_f[:], sem="d_bu2")

        t_wrf = dma("sync", BGF[0:BOT, 4, 128:176], wr_f[:], deps=[], sem="d_wr")
        t_wrb = vop(lambda e: e.tensor_copy(wr_b[:], BGF[0:BOT, 4, 128:176]), [t_wrf])
        t_akf = dma("sync", CAST1[0:BOT, :], ak_f[:], deps=[], sem="d_ak")
        t_akb = vop(lambda e: e.tensor_copy(ak_a[0:BOT, :], CAST1[0:BOT, :]), [t_akf])
        t_lrf = dma("sync", BGF[0:1, 4, 176:1200], lr_f[:], deps=[], sem="d_lr")
        t_lrb = vop(lambda e: e.tensor_copy(ak_a[BOT:BOT + 1, :], BGF[0:1, 4, 176:1200]),
                    [t_lrf])
        t_rqz = vop(lambda e: e.memset(rq_a[32:64, :], 0.0))
        t_rkz = vop(lambda e: e.memset(rk_a[32:64, :], 0.0))
        t_rklr = vop(lambda e: e.tensor_copy(rk_a[64:65, :],
                                             ak_a[BOT:BOT + 1, :]), [t_lrb, t_rkz])
        t_wu2f = dma("sync", BGF[:, 4, 1200:1208], wu2_f[:], deps=[], sem="d_wu2")
        t_wu2b = vop(lambda e: e.tensor_copy(wu2_b[:], BGF[:, 4, 1200:1208]), [t_wu2f])
        t_bof = dma("sync", XB[0][0:1, :], bo_f[:], deps=[], sem="d_boin")
        t_bob = vop(lambda e: e.tensor_copy(JUNK[0:1, 0:E], XB[0][0:1, :]), [t_bof])
        t_bos = dma("gpsimd", s_bo[:], JUNK[0:1, 0:E], deps=[t_bob], sem="d_bos")
        t_wqff = dma("sync", XB[1][:, 0:2048], wqf_f[:], deps=[], sem="d_wqf")
        t_wqfb = vop(lambda e: e.tensor_copy(WQF, XB[1][:, 0:2048]), [t_wqff])

        cast_jobs = []
        for j in range(HKV):
            cast_jobs.append((wk_f[j], s_wk[j], "wk", j))
        for j in range(HKV):
            cast_jobs.append((wv_f[j], s_wv[j], "wv", j))
        for j in range(HQ):
            cast_jobs.append((wq_f[j], s_wq[j], "wq", j))
        for j in range(8):
            cast_jobs.append((wu1_f[j], s_wu1[j], "wu1", j))
        for j in range(32):
            cast_jobs.append((av_f[j], s_av[j], "av", j))
        for j in range(32):
            cast_jobs.append((wvd_f[j], s_wvd[j], "wvd", j))
        for j in range(8):
            for rc in range(2):
                cast_jobs.append((wvu_f[j, rc], s_wvu[j, rc], "wvu", j))
        for j in range(8):
            cast_jobs.append((wo_f[j], s_wo[j], "wo", j))

        CASTB = [CAST0, CAST1]
        COUTB = [A[:, 0, 0:1024], A[:, 0, 1024:2048]]
        cin_free = [[t_eyeb, t_wrb], [t_akb, t_wu2b]]
        cout_free = [[], []]
        rdy = {}
        ci = 0
        for src, dst, tag, j in cast_jobs:
            total = src.shape[-2] * src.shape[-1] if len(src.shape) > 2 else None
            flat_src = src.rearrange("p f -> p f") if len(src.shape) == 2 else src
            free = src.shape[-1]
            ntile = (free + 1023) // 1024
            t_last = None
            for tt in range(ntile):
                f0 = tt * 1024
                f1 = min(free, f0 + 1024)
                w = f1 - f0
                bi = ci % 2
                ci += 1
                t_ld = dma("sync", CASTB[bi][:, 0:w], src[:, f0:f1], deps=cin_free[bi], sem=f"d_ci{bi}")
                if bi == 0:
                    t_c = sop(lambda e, o=COUTB[bi][:, 0:w], i=CASTB[bi][:, 0:w]:
                              e.copy(o, i), deps=[t_ld] + cout_free[bi])
                else:
                    t_c = vop(lambda e, o=COUTB[bi][:, 0:w], i=CASTB[bi][:, 0:w]:
                              e.tensor_copy(o, i), deps=[t_ld] + cout_free[bi])
                cin_free[bi] = [t_c]
                t_st = dma("gpsimd", dst[:, f0:f1], COUTB[bi][:, 0:w], deps=[t_c], sem=f"d_cst{bi}")
                cout_free[bi] = [t_st]
                t_last = t_st
            rdy[(tag, j)] = t_last
        t_wk_rdy = rdy[("wk", HKV - 1)]
        t_wv_rdy = rdy[("wv", HKV - 1)]
        t_wq_rdy = rdy[("wq", HQ - 1)]
        t_wu1_rdy = rdy[("wu1", 7)]
        t_av_rdy = rdy[("av", 31)]
        t_wvd_rdy = rdy[("wvd", 31)]
        t_wvu_rdy = rdy[("wvu", 7)]
        t_wo_rdy = rdy[("wo", 7)]


        t_lv = [None] * 64
        stat_free = [[], []]

        def chunk_stats(xb, t_ld, cc_lv, own_cc, prev_junk):
            si = (cc_lv % 2) * 8
            cs = [stat[:, si + k:si + k + 1] for k in range(8)]
            fr = stat_free[cc_lv % 2]
            t_sum = vop(lambda e: e.reduce_sum(cs[0], xb, axis=AX.X), [t_ld] + fr)
            t_sq2 = sop(lambda e: e.activation(JUNK, xb, AF.Square, accum_out=cs[2]),
                        [t_ld, prev_junk] + fr)
            t_mean = vop(lambda e: e.tensor_scalar(cs[1], cs[0], 1.0 / E, None,
                                                   ALU.mult), [t_sum])
            tn = tr = None
            if own_cc is not None:
                tn = vop(lambda e, cc=own_cc: e.tensor_scalar(
                    negm[:, cc:cc + 1], cs[1], -1.0, None, ALU.mult), [t_mean])
            t_msq = vop(lambda e: e.tensor_scalar(cs[3], cs[1], cs[1], None, ALU.mult),
                        [t_mean])
            t_var = vop(lambda e: e.scalar_tensor_tensor(
                cs[3], cs[2], 1.0 / E, cs[3], ALU.mult, ALU.subtract), [t_sq2, t_msq])
            t_veps = vop(lambda e: e.tensor_scalar(cs[4], cs[3], EPS, None, ALU.add),
                         [t_var])
            if own_cc is not None:
                t_sqr = sop(lambda e: e.activation(cs[5], cs[4], AF.Sqrt), [t_veps])
                tr = vop(lambda e, cc=own_cc: e.reciprocal(rstd[:, cc:cc + 1], cs[5]),
                         [t_sqr])
            if "lv" in ABL:
                tl = vop(lambda e, cc=cc_lv: e.memset(lv_all[:, cc:cc + 1], 0.5),
                         [])
                t_lv[cc_lv] = tl
                stat_free[cc_lv % 2] = [tl, t_veps] + ([tr] if tr else [])
                return tn, tr, tl, t_sq2
            t_rv = vop(lambda e: e.reciprocal(cs[6], cs[4]), [t_veps])
            t_vx = vop(lambda e: e.tensor_tensor(cs[7], cs[3], cs[6], ALU.mult),
                       [t_var, t_rv])
            tl = sop(lambda e, cc=cc_lv: e.activation(
                lv_all[:, cc:cc + 1], cs[7], AF.Ln, bias=1.0), [t_vx])
            t_lv[cc_lv] = tl
            stat_free[cc_lv % 2] = [tl] + ([tr] if tr else [])
            return tn, tr, tl, t_sq2

        xb_free = [[t_bob], [t_wqfb]]
        xcb_free = []
        dmat_free = []
        prev_junk = t_bos
        t_xpx_st = []
        t_q_st, t_k_st, t_v_st = [], [], []
        t_h1_ev = []
        t_q8_ev = []
        pan_free = [[], []]
        stgK_free, stgV_free, stgQ_free = [], [], []

        for sl in range(NSL):
            t_xnt_ev = []
            for j in range(4):
                cc = 4 * sl + j
                bi = cc % 2
                xb = XB[bi]
                t_ld = dma("sync", xb, x_own[cc],
                           deps=xb_free[bi], sem=f"d_xb{bi}")
                tn, tr, tl, prev_junk = chunk_stats(xb, t_ld, cc, cc, prev_junk)
                new_xb_free = [tl, prev_junk]
                t_xcb = vop(lambda e, xb=xb, cc=cc: e.tensor_scalar(
                    XCB, xb, negm[:, cc:cc + 1], None, ALU.add),
                    [t_ld, tn] + xcb_free)
                new_xb_free.append(t_xcb)
                if cc < OWN_CH and "xpx" not in ABL:
                    t_a1 = vop(lambda e, cc=cc: e.tensor_scalar(
                        a01[:, 1:2], rstd[:, cc:cc + 1], 1.0, None, ALU.add), [tr])
                    t_a0 = vop(lambda e, cc=cc: e.tensor_tensor(
                        a01[:, 0:1], negm[:, cc:cc + 1], rstd[:, cc:cc + 1],
                        ALU.mult), [tn, tr])
                    t_p1 = vop(lambda e, xb=xb: e.tensor_scalar(
                        xb, xb, a01[:, 1:2], None, ALU.mult),
                        [t_a1, t_xcb, tl, prev_junk])
                    t_p2 = vop(lambda e, xb=xb: e.tensor_scalar(
                        xb, xb, a01[:, 0:1], None, ALU.add), [t_a0, t_p1])
                    t_xst = dma("gpsimd", s_xpx[cc], xb,
                                deps=[t_p2], sem="d_xpxs")
                    t_xpx_st.append(t_xst)
                    new_xb_free = [t_xst]
                xb_free[bi] = new_xb_free
                t_dm = vop(lambda e, cc=cc: e.tensor_scalar(
                    dmat[:], eyeb[:], rstd[:, cc:cc + 1], None, ALU.mult),
                    [t_eyeb, tr] + dmat_free)
                tms = []
                for ec in range(32):
                    bnk = ec % 4
                    t_m = mm(G[bnk][:, 0:128], XCB[:, ec * 128:(ec + 1) * 128],
                             dmat[:], True, True, deps=[t_xcb, t_dm, gfree[bnk]])
                    dst = XNT[:, ec, j * 128:(j + 1) * 128]
                    src = G[bnk][:, 0:128]
                    if ec % 2 == 0:
                        t_e = sop(lambda e, d=dst, s=src: e.copy(d, s), [t_m])
                    else:
                        t_e = vop(lambda e, d=dst, s=src: e.tensor_copy(d, s), [t_m])
                    gfree[bnk] = t_e
                    tms.append(t_m)
                    t_xnt_ev.append(t_e)
                xcb_free = [tms[-1]]
                dmat_free = [tms[-1]]

            def proj32(pan, bnk, first_deps, out_ap=None):
                t_last = None
                oap = G[bnk][:, 0:512] if out_ap is None else out_ap
                for ec in range(32):
                    dep = first_deps if ec == 0 else []
                    t_last = mm(oap, pan[:, ec * 128:(ec + 1) * 128],
                                XNT[:, ec, :], ec == 0, ec == 31, deps=dep)
                return t_last

            for hc in range(HKV):
                bi = hc % 2
                pan = A[:, 2 + bi, :]
                t_pl = dma("sync", pan, s_wk[hc], deps=pan_free[bi] + [t_wk_rdy], sem=f"d_pan{bi}")
                bnk = 4 + (hc % 2)
                t_mmE = proj32(pan, bnk, [t_pl, gfree[bnk]] + t_xnt_ev)
                t_ev = sop(lambda e, b=bnk: e.copy(STG_K, G[b][:, 0:512]),
                           [t_mmE] + stgK_free)
                t_st = dma("gpsimd", s_k[hc][:, sl * 512:(sl + 1) * 512], STG_K,
                           deps=[t_ev], sem="d_kst")
                gfree[bnk] = t_ev
                stgK_free = [t_st]
                pan_free[bi] = [t_mmE]
                t_k_st.append(t_st)

            for hc in ([] if "vproj" in ABL else range(HKV)):
                bi = hc % 2
                pan = A[:, 2 + bi, :]
                t_pl = dma("sync", pan, s_wv[hc], deps=pan_free[bi] + [t_wv_rdy], sem=f"d_pan{bi}")
                bnk = 4 + (hc % 2)
                t_mmE = proj32(pan, bnk, [t_pl, gfree[bnk]] + t_xnt_ev)
                t_ev = sop(lambda e, b=bnk: e.copy(STG_V, G[b][:, 0:512]),
                           [t_mmE] + stgV_free)
                t_st = dma("gpsimd", s_vd[hc][:, sl * 512:(sl + 1) * 512], STG_V,
                           deps=[t_ev], sem="d_vst")
                gfree[bnk] = t_ev
                stgV_free = [t_st]
                pan_free[bi] = [t_mmE]
                t_v_st.append(t_st)

            if sl < 2:
                for qh in range(HQ):
                    bi = qh % 2
                    pan = A[:, 2 + bi, :]
                    t_pl = dma("sync", pan, s_wq[qh], deps=pan_free[bi] + [t_wq_rdy], sem=f"d_pan{bi}")
                    bnk = 4 + (qh % 2)
                    t_mmE = proj32(pan, bnk, [t_pl, gfree[bnk]] + t_xnt_ev)
                    t_ev = sop(lambda e, b=bnk: e.copy(STG_Q, G[b][:, 0:512]),
                               [t_mmE] + stgQ_free)
                    t_st = dma("gpsimd", s_q[qh][:, sl * 512:(sl + 1) * 512], STG_Q,
                               deps=[t_ev], sem="d_qst")
                    gfree[bnk] = t_ev
                    stgQ_free = [t_st]
                    pan_free[bi] = [t_mmE]
                    t_q_st.append(t_st)
                for uc in ([] if "wu1" in ABL else range(8)):
                    bi = uc % 2
                    pan = A[:, 2 + bi, :]
                    t_pl = dma("sync", pan, s_wu1[uc], deps=pan_free[bi] + [t_wu1_rdy], sem=f"d_pan{bi}")
                    bnk = 4 + (uc % 2)
                    t_mmE = proj32(pan, bnk, [t_pl, gfree[bnk]] + t_xnt_ev)
                    t_ev = sop(lambda e, b=bnk, uc=uc, sl=sl: e.activation(
                        H1[:, uc, sl * 512:(sl + 1) * 512], G[b][:, 0:512],
                        AF.Gelu_apprx_tanh, bias=bu1_s[:, uc:uc + 1]),
                        [t_mmE, t_bu1])
                    gfree[bnk] = t_ev
                    pan_free[bi] = [t_mmE]
                    t_h1_ev.append(t_ev)
                if "query" in ABL:
                    t_ev = vop(lambda e, sl=sl: e.memset(
                        q8[0:BOT, sl * 512:(sl + 1) * 512], 0.01), [])
                else:
                    bnk = 4
                    t_mmq = None
                    for ec in range(32):
                        dep = ([t_wqfb, gfree[bnk]] + t_xnt_ev) if ec == 0 else []
                        t_mmq = mm(G[bnk][0:BOT, 0:512],
                                   WQF[:, ec * BOT:(ec + 1) * BOT],
                                   XNT[:, ec, :], ec == 0, ec == 31, deps=dep)
                    t_ev = vop(lambda e, sl=sl, b=bnk: e.tensor_scalar(
                        q8[0:BOT, sl * 512:(sl + 1) * 512], G[b][0:BOT, 0:512],
                        0.125, None, ALU.mult), [t_mmq])
                    gfree[bnk] = t_ev
                t_q8_ev.append(t_ev)

            for j2 in range(4):
                cc2 = 4 * sl + j2
                bi = cc2 % 2
                xb = XB[bi]
                t_ld = dma("sync", xb, x_oth[cc2],
                           deps=xb_free[bi], sem=f"d_xb{bi}")
                _, _, tl2, prev_junk = chunk_stats(xb, t_ld, NCH + cc2, None,
                                                   prev_junk)
                xb_free[bi] = [tl2, prev_junk]


        pass

        bnk = 5
        t_gmm = mm(G[bnk][0:1, 0:64], ones_f[:], lv_all[:, 0:64], True, True,
                   deps=[t_onesf, gfree[bnk]] + t_lv)
        t_gcp = vop(lambda e, b=bnk: e.tensor_copy(gs_row[:], G[b][0:1, 0:64]), [t_gmm])
        gfree[bnk] = t_gcp
        t_gred = vop(lambda e: e.reduce_sum(gs_one[0:1, 0:1], gs_row[:], axis=AX.X),
                     [t_gcp])
        t_gm = vop(lambda e: e.tensor_scalar(gs_one[0:1, 1:2], gs_one[0:1, 0:1],
                                             1.0 / (B * T), 1e-6, ALU.mult,
                                             op1=ALU.add), [t_gred])
        t_gi = vop(lambda e: e.reciprocal(gs_one[0:1, 2:3], gs_one[0:1, 1:2]), [t_gm])
        t_gb0 = dma("gpsimd", d_bc[:], gs_one[0:1, 2:3], deps=[t_gi], sem="d_gbc")
        t_gbc = dma("gpsimd", ginv_b[:], d_bc[:].to_broadcast([128, 1]), deps=[t_gb0], sem="d_gbc")

        t_sgr = []
        for th in range(2):
            bnk = 5
            t_mml = None
            for uc in range(8):
                dep = ([t_wu2b, gfree[bnk]] + t_h1_ev) if uc == 0 else []
                t_mml = mm(G[bnk][0:1, 0:512], wu2_b[:, uc:uc + 1],
                           H1[:, uc, th * 512:(th + 1) * 512], uc == 0, uc == 7,
                           deps=dep)
            t_sgv = sop(lambda e, th=th, b=bnk: e.activation(
                sgrow[0:1, th * 512:(th + 1) * 512], G[b][0:1, 0:512],
                AF.Sigmoid, bias=bu2_s[0:1, 0:1]), [t_mml, t_bu2])
            gfree[bnk] = t_sgv
            t_sgr.append(t_sgv)
        t_sgt = []
        for tc in range(OWN_CH):
            bnk = 5
            t_mt = mm(G[bnk][0:128, 0:1], sgrow[0:1, tc * 128:(tc + 1) * 128],
                      ones_b[0:1, 0:1], True, True,
                      deps=[t_onesb, gfree[bnk]] + t_sgr)
            t_e = vop(lambda e, tc=tc, b=bnk: e.tensor_copy(
                sgt[:, tc:tc + 1], G[b][0:128, 0:1]), [t_mt])
            gfree[bnk] = t_e
            t_sgt.append(t_e)

        t_gate = []
        gtmp_free = []
        for tc in range(OWN_CH):
            g0 = gtmp[:, 0:1]
            g1 = gtmp[:, 1:2]
            t_nv = vop(lambda e, tc=tc: e.tensor_scalar(
                g0, lv_all[:, tc:tc + 1], ginv_b[:, 0:1], 0.5, ALU.mult,
                op1=ALU.mult), [t_gbc, t_lv[tc]] + gtmp_free)
            t_u = vop(lambda e, tc=tc: e.scalar_tensor_tensor(
                g1, sgt[:, tc:tc + 1], 2.5, g0, ALU.mult, ALU.add),
                [t_nv, t_sgt[tc]])
            t_u2 = vop(lambda e: e.tensor_scalar(g1, g1, 5.0, 0.0, ALU.min,
                                                 op1=ALU.max), [t_u])
            t_un = vop(lambda e: e.tensor_scalar(
                g1, g1, -TAU_LOW, 1.0 / (TAU_HIGH - TAU_LOW), ALU.add,
                op1=ALU.mult), [t_u2])
            t_un2 = vop(lambda e: e.tensor_scalar(g1, g1, 1.0, 0.0, ALU.min,
                                                  op1=ALU.max), [t_un])
            t_gl = vop(lambda e: e.tensor_scalar(g1, g1, gw1_b[:, 0:1], None,
                                                 ALU.mult), [t_un2, t_gw1])
            t_gl2 = vop(lambda e: e.tensor_scalar(g1, g1, gb_b[:, 0:1], None,
                                                  ALU.add), [t_gl, t_gb])
            t_sig = sop(lambda e: e.activation(g0, g1, AF.Sigmoid), [t_gl2])
            t_msk = vop(lambda e: e.tensor_scalar(g1, g0, EARLY_EXIT, None,
                                                  ALU.is_ge), [t_sig])
            t_g = vop(lambda e, tc=tc: e.tensor_tensor(
                gate[:, tc:tc + 1], g0, g1, ALU.mult), [t_msk])
            gtmp_free = [t_g]
            t_gate.append(t_g)

        t_q8o = vop(lambda e: e.memset(q8[BOT:BOT + 1, :], 1.0))
        t_rq = []
        for th in range(2):
            bnk = 5
            t_m = mm(G[bnk][0:RDIM, 0:512], wr_b[:],
                     q8[0:BOT, th * 512:(th + 1) * 512], True, True,
                     deps=[t_wrb, gfree[bnk]] + t_q8_ev)
            t_e = vop(lambda e, th=th, b=bnk: e.tensor_scalar(
                rq_a[0:RDIM, th * 512:(th + 1) * 512], G[b][0:RDIM, 0:512],
                8.0 / math.sqrt(RDIM), None, ALU.mult), [t_m, t_rqz])
            gfree[bnk] = t_e
            t_rq.append(t_e)
        t_rqo = vop(lambda e: e.memset(rq_a[64:65, :], 1.0), [t_rqz])
        t_rk = [t_rklr]
        for th in range(2):
            bnk = 5
            t_m = mm(G[bnk][0:RDIM, 0:512], wr_b[:],
                     ak_a[0:BOT, th * 512:(th + 1) * 512], True, True,
                     deps=[t_wrb, t_akb, gfree[bnk]])
            t_e = sop(lambda e, th=th, b=bnk: e.copy(
                rk_a[0:RDIM, th * 512:(th + 1) * 512], G[b][0:RDIM, 0:512]),
                [t_m, t_rkz])
            gfree[bnk] = t_e
            t_rk.append(t_e)

        ffn_state = {
            "pw_free": [], "srt_free": list(xb_free[0]), "ach_free": list(xb_free[0]),
            "tbf_free": list(xb_free[0]), "srw_free": list(xb_free[0]),
            "t_sr": [], "t_ach": [], "t_w": None, "t_m8": None,
        }

        def emit_ffn(i, ybnk, t_slot):
            fs = ffn_state
            tc, st = i // 8, i % 8
            c_m = gtmp[:, 2:3]
            bankfree = t_slot
            if st == 0:
                fs["t_sr"] = []
                fs["t_ach"] = []
                t_m = mm(G[ybnk][:, 0:512], rq_a[:, tc * 128:(tc + 1) * 128],
                         rk_a[:, 0:512], True, True,
                         deps=[t_rqo, t_slot] + t_rq + t_rk)
                t_e = sop(lambda e, b=ybnk: e.copy(SRT[:, 0:512], G[b][:, 0:512]),
                          [t_m] + fs["srt_free"])
                fs["srt_free"] = []
                fs["t_sr"].append(t_e)
                bankfree = t_e
            elif st == 1:
                t_m = mm(G[ybnk][:, 0:512], rq_a[:, tc * 128:(tc + 1) * 128],
                         rk_a[:, 512:1024], True, True,
                         deps=[t_rqo, t_slot] + t_rq + t_rk)
                t_e = sop(lambda e, b=ybnk: e.copy(SRT[:, 512:1024], G[b][:, 0:512]),
                          [t_m])
                fs["t_sr"].append(t_e)
                fs["t_w"] = vop(lambda e: e.tensor_copy(SRW[:], SRT[:]),
                                fs["t_sr"] + fs["srw_free"])
                fs["srw_free"] = []
                bankfree = t_e
            elif st == 2:
                t_m = mm(G[ybnk][:, 0:512], q8[:, tc * 128:(tc + 1) * 128],
                         ak_a[:, 0:512], True, True,
                         deps=[t_q8o, t_akb, t_lrb, t_slot] + t_q8_ev)
                t_e = sop(lambda e, b=ybnk: e.copy(ACH[:, 0:512], G[b][:, 0:512]),
                          [t_m] + fs["ach_free"])
                fs["ach_free"] = []
                fs["t_ach"].append(t_e)
                bankfree = t_e
                t_mx = vop(lambda e: e.reduce_max(c_m, SRW[:], axis=AX.X),
                           [fs["t_w"]])
                t_tb = vop(lambda e: e.tensor_scalar(
                    TBF[:], SRW[:], c_m, BIGC, ALU.is_lt, op1=ALU.mult),
                    [t_mx] + fs["tbf_free"])
                fs["tbf_free"] = []
                fs["t_w"] = vop(lambda e: e.scalar_tensor_tensor(
                    SRW[:], TBF[:], HBIG, SRW[:], ALU.subtract, ALU.min), [t_tb])
            elif st == 3:
                t_m = mm(G[ybnk][:, 0:512], q8[:, tc * 128:(tc + 1) * 128],
                         ak_a[:, 512:1024], True, True,
                         deps=[t_q8o, t_akb, t_lrb, t_slot] + t_q8_ev)
                t_e = sop(lambda e, b=ybnk: e.copy(ACH[:, 512:1024], G[b][:, 0:512]),
                          [t_m])
                fs["t_ach"].append(t_e)
                bankfree = t_e
                for _ in range(2):
                    t_mx = vop(lambda e: e.reduce_max(c_m, SRW[:], axis=AX.X),
                               [fs["t_w"]])
                    t_tb = vop(lambda e: e.tensor_scalar(
                        TBF[:], SRW[:], c_m, BIGC, ALU.is_lt, op1=ALU.mult), [t_mx])
                    fs["t_w"] = vop(lambda e: e.scalar_tensor_tensor(
                        SRW[:], TBF[:], HBIG, SRW[:], ALU.subtract, ALU.min), [t_tb])
            elif st in (4, 5):
                for _ in range(2):
                    t_mx = vop(lambda e: e.reduce_max(c_m, SRW[:], axis=AX.X),
                               [fs["t_w"]])
                    t_tb = vop(lambda e: e.tensor_scalar(
                        TBF[:], SRW[:], c_m, BIGC, ALU.is_lt, op1=ALU.mult), [t_mx])
                    fs["t_w"] = vop(lambda e: e.scalar_tensor_tensor(
                        SRW[:], TBF[:], HBIG, SRW[:], ALU.subtract, ALU.min), [t_tb])
            elif st == 6:
                t_mx = vop(lambda e: e.reduce_max(c_m, SRW[:], axis=AX.X),
                           [fs["t_w"]])
                t_tb = vop(lambda e: e.tensor_scalar(
                    TBF[:], SRW[:], c_m, BIGC, ALU.is_lt, op1=ALU.mult), [t_mx])
                t_w = vop(lambda e: e.scalar_tensor_tensor(
                    SRW[:], TBF[:], HBIG, SRW[:], ALU.subtract, ALU.min), [t_tb])
                t_m8 = vop(lambda e: e.reduce_max(c_m, SRW[:], axis=AX.X), [t_w])
                fs["srw_free"] = [t_m8]
                t_tb2 = vop(lambda e: e.tensor_scalar(
                    TBF[:], SRT[:], c_m, BIGC, ALU.is_lt, op1=ALU.mult),
                    [t_m8] + fs["t_sr"])
                fs["srt_free"] = [t_tb2]
                t_akn = vop(lambda e: e.scalar_tensor_tensor(
                    ACH[:], TBF[:], HBIG, ACH[:], ALU.subtract, ALU.min),
                    [t_tb2] + fs["t_ach"])
                fs["tbf_free"] = [t_akn]
                t_am = vop(lambda e: e.reduce_max(gtmp[:, 3:4], ACH[:], axis=AX.X),
                           [t_akn])
                fs["t_nam"] = vop(lambda e: e.tensor_scalar(
                    gtmp[:, 2:3], gtmp[:, 3:4], -1.0, None, ALU.mult), [t_am])
                fs["t_akn"] = t_akn
            else:
                wi = tc % 2
                t_exp = sop(lambda e, wi=wi: e.activation(
                    pw_t[:], ACH[:], AF.Exp, bias=gtmp[:, 2:3],
                    accum_out=wsum[:, wi:wi + 1]),
                    [fs["t_nam"], fs["t_akn"]] + fs["pw_free"])
                fs["ach_free"] = [t_exp]
                t_wi = vop(lambda e, wi=wi: e.reciprocal(
                    winv[:, wi:wi + 1], wsum[:, wi:wi + 1]), [t_exp])
                t_sc2 = vop(lambda e, wi=wi, tc=tc: e.tensor_tensor(
                    winv[:, wi:wi + 1], winv[:, wi:wi + 1], gate[:, tc:tc + 1],
                    ALU.mult), [t_wi, t_gate[tc]])
                t_pws = vop(lambda e, wi=wi: e.tensor_scalar(
                    pw_t[:], pw_t[:], winv[:, wi:wi + 1], None, ALU.mult),
                    [t_sc2, t_exp])
                t_pst = dma("gpsimd", s_pw[tc], pw_t[:], deps=[t_pws], sem="d_pw")
                fs["pw_free"] = [t_pst]
                fs["t_pw_st"] = fs.get("t_pw_st", []) + [t_pst]
            return bankfree

        kp_free = [cout_free[0] + cout_free[1], []]
        vp_free = [list(pan_free[0]), list(pan_free[1])]
        vk_free = [[], [t_k_st[-1]]]
        qb_free = [[t_v_st[-1] if t_v_st else None], [t_q_st[-1] if t_q_st else None]]
        pt_carry = [None] * 6
        denb_free = []
        denr_free = []
        t_ye_all = []
        for hkv in range(HKV):
            bi = hkv % 2
            kp = A[:, 0 + bi, :]
            vp = A[:, 2 + bi, :]
            t_kl = dma("sync", kp, s_k[hkv], deps=kp_free[bi] + t_k_st, sem=f"d_kp{bi}")
            t_vl = dma("sync", vp, s_vd[hkv], deps=vp_free[bi] + t_v_st, sem=f"d_vp{bi}")
            vk = A[:, 4 + bi, :]
            t_vk_ev = []
            for kc in range(32):
                bnk = 2 if kc % 2 == 0 else 3
                t_m = mm(G[bnk][:, 0:128], vp[:, kc * 128:(kc + 1) * 128], eyeb[:],
                         True, True,
                         deps=[t_vl, t_eyeb, gfree[bnk]] + vk_free[bi])
                dst = vk[:, kc * 128:(kc + 1) * 128]
                src = G[bnk][:, 0:128]
                if kc % 2 == 0:
                    t_e = sop(lambda e, d=dst, s=src: e.copy(d, s), [t_m])
                else:
                    t_e = vop(lambda e, d=dst, s=src: e.tensor_copy(d, s), [t_m])
                gfree[bnk] = t_e
                t_vk_ev.append(t_e)
            vk_free[bi] = []
            vp_free[bi] = [t_vk_ev[-1]]
            t_last_av = None
            for g in range(4):
                h = hkv * 4 + g
                t_ql = [dma("sync", QB[qi], s_q[h][:, qi * 512:(qi + 1) * 512],
                            deps=qb_free[qi] + t_q_st, sem=f"d_qb{qi}")
                        for qi in range(2)]
                SBKs = [[0, 1], [4, 5]]
                YB = [2, 3]
                exp_hist = [[None] * 40, [None] * 40]
                cons_hist = [[None] * 40, [None] * 40]
                for st2 in range(2):
                    for k in range(3):
                        cons_hist[st2][k] = pt_carry[3 * st2 + k]
                t_s_last = [None, None]
                t_y_last = [None, None]
                t_d_last = [None, None]
                t_ex_all = [[None] * 32, [None] * 32]

                def emit_S(kc, st2):
                    sbnk = SBKs[st2][kc % 2]
                    pti = 3 * st2 + kc % 3
                    t_s = mm(G[sbnk][:, 0:512], kp[:, kc * 128:(kc + 1) * 128],
                             QB[st2], True, True,
                             deps=[t_kl, t_ql[st2], gfree[sbnk],
                                   exp_hist[st2][kc]])
                    t_ex = sop(lambda e, sb=sbnk, pi=pti: e.activation(
                        PT[pi], G[sb][:, 0:512], AF.Exp, scale=ISQD),
                        [t_s, cons_hist[st2][kc]])
                    exp_hist[st2][kc + 2] = t_ex
                    t_ex_all[st2][kc] = t_ex
                    t_s_last[st2] = t_s

                def emit_YD(kc, st2):
                    pti = 3 * st2 + kc % 3
                    ybnk = YB[st2]
                    t_ex = t_ex_all[st2][kc]
                    ydep = [t_ex]
                    if kc == 0:
                        ydep += [gfree[ybnk], t_vk_ev[-1], t_vk_ev[-2]]
                    t_y = mm(G[ybnk][:, 0:512], vk[:, kc * 128:(kc + 1) * 128],
                             PT[pti], kc == 0, kc == 31, deps=ydep)
                    ddep = [t_ex]
                    if kc == 0:
                        ddep += [pdfree[st2], t_onesb]
                    t_d = mm(PD[st2][0:1, 0:512], ones_b[:], PT[pti],
                             kc == 0, kc == 31, deps=ddep)
                    cons_hist[st2][kc + 3] = t_d
                    t_y_last[st2] = t_y
                    t_d_last[st2] = t_d

                for kc in range(32):
                    emit_S(kc, 0)
                    emit_S(kc, 1)
                    if kc >= 2:
                        emit_YD(kc - 2, 0)
                        emit_YD(kc - 2, 1)
                for kc in (30, 31):
                    emit_YD(kc, 0)
                    emit_YD(kc, 1)
                gfree[0] = exp_hist[0][32]
                gfree[1] = exp_hist[0][33]
                gfree[4] = exp_hist[1][32]
                gfree[5] = exp_hist[1][33]
                pt_carry = [None] * 6
                for st2 in range(2):
                    for s in range(3):
                        kc = [k for k in (29, 30, 31) if k % 3 == s][0]
                        pt_carry[3 * st2 + s] = cons_hist[st2][kc + 3]
                for st2 in range(2):
                    ybnk = YB[st2]
                    t_r = vop(lambda e, p=st2: e.reciprocal(den_r[:], PD[p][0:1, 0:512]),
                              [t_d_last[st2]] + denr_free)
                    pdfree[st2] = t_r
                    t_b0 = dma("gpsimd", d_den[:], den_r[:], deps=[t_r], sem="d_den")
                    t_b1 = dma("gpsimd", den_b[:], d_den[:].to_broadcast([128, 512]),
                               deps=[t_b0] + denb_free, sem="d_den")
                    denr_free = [t_b0]
                    t_ye = vop(lambda e, h=h, qt=st2, b=ybnk: e.tensor_tensor(
                        V[:, h, qt * 512:(qt + 1) * 512], G[b][:, 0:512], den_b[:],
                        ALU.mult), [t_b1, t_y_last[st2]])
                    denb_free = [t_ye]
                    qb_free[st2] = [t_s_last[st2]]
                    t_ye_all.append(t_ye)
                    t_last_av = t_y_last[st2]
                    gfree[ybnk] = emit_ffn(hkv * 8 + g * 2 + st2, ybnk, t_ye)
            kp_free[bi] = [t_last_av]
            vk_free[bi] = [t_last_av]

        pwl_free = list(ffn_state["pw_free"])
        t_wt_ev = []
        for tc in range(OWN_CH):
            t_pl = dma("sync", pw_t[:], s_pw[tc],
                       deps=pwl_free + ffn_state["t_pw_st"], sem="d_pwl")
            for scu in range(8):
                bnk = scu % 4
                t_m = mm(G[bnk][:, 0:128], pw_t[:, scu * 128:(scu + 1) * 128],
                         eyeb[:], True, True, deps=[t_pl, t_eyeb, gfree[bnk],
                                                    t_last_av if tc == 0 else None])
                dst = WTap(scu, tc * 128, (tc + 1) * 128)
                srcp = G[bnk][:, 0:128]
                if scu % 2 == 0:
                    t_e = sop(lambda e, d=dst, s=srcp: e.copy(d, s), [t_m])
                else:
                    t_e = vop(lambda e, d=dst, s=srcp: e.tensor_copy(d, s), [t_m])
                gfree[bnk] = t_e
                t_wt_ev.append(t_e)
                t_lastm = t_m
            pwl_free = [t_lastm]

        avp_free = [[t_last_av], [t_last_av]]
        vdp_free = [[], []]
        ao_free = [t_last_av]
        t_t1m = [None] * 4
        for hc in range(32):
            bi = hc % 2
            avp = A[:, 2 + bi, 0:1024]
            t_avl = dma("sync", avp, s_av[hc], deps=avp_free[bi] + [t_av_rdy], sem=f"d_pan{bi}")
            vdp = wvd_b[:, bi, :]
            t_vdl = dma("sync", vdp, s_wvd[hc], deps=vdp_free[bi] + [t_wvd_rdy], sem=f"d_vdp{bi}")
            t_am2 = [None, None]
            for th in range(2):
                bnk = 4 + th
                t_last = None
                for scu in range(8):
                    dep = ([t_avl, gfree[bnk]] + t_wt_ev) if scu == 0 else []
                    t_last = mm(G[bnk][:, 0:512], avp[:, scu * 128:(scu + 1) * 128],
                                WTap(scu, th * 512, (th + 1) * 512),
                                scu == 0, scu == 7, deps=dep)
                t_am2[th] = t_last
            t_ae = sop(lambda e: e.copy(AOT[:, 0:512], G[4][:, 0:512]),
                       [t_am2[0]] + ao_free)
            t_ae2 = sop(lambda e: e.copy(AOT[:, 512:1024], G[5][:, 0:512]),
                        [t_am2[1]])
            gfree[4] = t_ae
            gfree[5] = t_ae2
            avp_free[bi] = [t_am2[1]]
            for rc in range(2):
                for th in range(2):
                    bnk = 2 * rc + th
                    dep = [t_vdl, t_ae, t_ae2]
                    if hc == 0:
                        dep.append(gfree[bnk])
                    t_t1m[bnk] = mm(G[bnk][:, 0:512],
                                    vdp[:, rc * 128:(rc + 1) * 128],
                                    AOT[:, th * 512:(th + 1) * 512],
                                    hc == 0, hc == 31, deps=dep)
            vdp_free[bi] = [t_t1m[3]]
            ao_free = [t_t1m[3]]
        t_t1ev = []
        for rc in range(2):
            for th in range(2):
                bnk = 2 * rc + th
                t_e = sop(lambda e, rc=rc, th=th, b=bnk: e.copy(
                    t1[:, rc, th * 512:(th + 1) * 512], G[b][:, 0:512]),
                    [t_t1m[bnk]])
                gfree[bnk] = t_e
                t_t1ev.append(t_e)

        wo_free = [[t_t1ev[-1]], [t_t1ev[-1]]]
        wvu_free = list(t_ye_all[-1:])
        bo1_free = []
        xp_free = [[], []]
        os_free = [[], [], [], []]
        t_out_st = []
        for osl in range(8):
            obi = osl % 2
            t_wol = []
            for r in range(4):
                t_wol.append(dma("sync", A[:, 4 * obi + r, :],
                                 s_wo[osl][:, r * 4096:(r + 1) * 4096],
                                 deps=wo_free[obi] + [t_wo_rdy]
                                 + (t_ye_all if osl < 2 else []), sem=f"d_wol{obi}"))
            t_wvl = []
            for rc in range(2):
                t_wvl.append(dma("sync", wvu_t[:, rc, :],
                                 s_wvu[osl, rc], deps=[t_wvu_rdy] + wvu_free,
                                 sem="d_wvl"))
            t_bol = dma("sync", bo1_t[:], s_bo[0:1, osl * 512:(osl + 1) * 512],
                        deps=[t_bos, prev_junk] + bo1_free, sem="d_bol")
            last_mm = None
            for tc in range(OWN_CH):
                bnk = tc % 4
                t_l = None
                for qh in range(HQ):
                    dep = []
                    if qh == 0:
                        dep = [gfree[bnk], t_ye_all[-1]] + t_wol
                    t_l = mm(G[bnk][:, 0:512],
                             V[:, qh, tc * 128:(tc + 1) * 128],
                             A[:, 4 * obi + qh // 8, (qh % 8) * 512:(qh % 8 + 1) * 512],
                             qh == 0, False, deps=dep)
                for rc in range(2):
                    t_l = mm(G[bnk][:, 0:512], t1[:, rc, tc * 128:(tc + 1) * 128],
                             wvu_t[:, rc, :], False, False,
                             deps=t_wvl + (t_t1ev if tc == 0 else []))
                t_l = mm(G[bnk][:, 0:512], ones1b[:], bo1_t[:], False, True,
                         deps=[t_ones1b, t_bol])
                xi = tc % 2
                oi = tc % 4
                t_xl = dma("sync", xpxs[:, xi, :],
                           s_xpx[tc][:, osl * 512:(osl + 1) * 512],
                           deps=xp_free[xi] + t_xpx_st, sem=f"d_xpl{xi}")
                t_ev = vop(lambda e, b=bnk, xi=xi, oi=oi: e.tensor_tensor(
                    ostg[:, oi, :], G[b][:, 0:512], xpxs[:, xi, :], ALU.add),
                    [t_l, t_xl] + os_free[oi])
                gfree[bnk] = t_ev
                xp_free[xi] = [t_ev]
                t_st = dma("gpsimd", y_out[tc][:, osl * 512:(osl + 1) * 512],
                           ostg[:, oi, :], deps=[t_ev], sem=f"d_ost{oi}")
                os_free[oi] = [t_st]
                t_out_st.append(t_st)
                last_mm = t_l
            wo_free[obi] = [last_mm]
            wvu_free = [last_mm]
            bo1_free = [last_mm]

        if DBG:
            t_dd = []
            def dump_via(dst_ap, src_ap, deps, prevtok):
                n_p = src_ap.shape[0]
                t_c = vop(lambda e, s=src_ap, n=n_p: e.tensor_copy(ostg[0:n, 0, 0:s.shape[-1]], s),
                          deps + prevtok)
                t_d = dma("gpsimd", dst_ap, ostg[0:n_p, 0, 0:src_ap.shape[-1]], deps=[t_c], sem="d_dump")
                return t_d
            prevt = list(os_free[0] + os_free[1])
            t_d1 = dump_via(dbg_gate[:], gate[:], t_gate, prevt); prevt = [t_d1]
            t_d2 = dump_via(dbg_sgt[:], sgt[:], t_sgt, prevt); prevt = [t_d2]
            t_dd += [t_d1, t_d2]
            t_dd.append(dma("gpsimd", dbg_lv[:], lv_all[:], deps=t_lv, sem="d_dump"))
            t_dd.append(dma("gpsimd", dbg_gs[:], gs_one[:], deps=[t_gi], sem="d_dump"))
            t_dd.append(dma("gpsimd", dbg_nm[:], negm[:], deps=[], sem="d_dump"))
            t_dd.append(dma("gpsimd", dbg_rs[:], rstd[:], deps=[], sem="d_dump"))
            for rc in range(2):
                for th in range(2):
                    t_dx = dump_via(dbg_t1[:, rc, th * 512:(th + 1) * 512],
                                    t1[:, rc, th * 512:(th + 1) * 512], t_t1ev, prevt)
                    prevt = [t_dx]; t_dd.append(t_dx)
            for th in range(2):
                t_dx = dump_via(dbg_q8[:, th * 512:(th + 1) * 512],
                                q8[:, th * 512:(th + 1) * 512], t_q8_ev, prevt)
                prevt = [t_dx]; t_dd.append(t_dx)
            t_dd.append(dma("gpsimd", dbg_wsum[:], wsum[:], deps=[], sem="d_dump"))
            for th in range(2):
                t_dx = dump_via(dbg_yt[:, th * 512:(th + 1) * 512],
                                V[:, 0, th * 512:(th + 1) * 512], t_ye_all, prevt)
                prevt = [t_dx]; t_dd.append(t_dx)
            for hh in range(HKV):
                t_dd.append(dma("gpsimd", dbg_k[hh], s_k[hh], deps=t_k_st, sem="d_dump"))
                t_dd.append(dma("gpsimd", dbg_v[hh], s_vd[hh], deps=t_v_st, sem="d_dump"))
            for qh in range(HQ):
                t_dd.append(dma("gpsimd", dbg_q[qh], s_q[qh], deps=t_q_st, sem="d_dump"))
            for cc in range(OWN_CH):
                t_dd.append(dma("gpsimd", dbg_xpx[cc], s_xpx[cc], deps=t_xpx_st, sem="d_dump"))
            t_out_st = t_out_st + t_dd
        sch.op("gpsimd", lambda e: e.memset(ones1b[:], 1.0), deps=t_out_st)

        sems = {s: es_.enter_context(nc.semaphore(s)) for s in sorted(sch.semnames)}
        block = es_.enter_context(nc.Block())
        sch.emit(block, sems)

    return nc


def kernel(**inp):
    x = np.asarray(inp["L0_attn_x"], dtype=np.float32)
    Wq = np.asarray(inp["L0_attn_Wq"], dtype=np.float32)
    Wk = np.asarray(inp["L0_attn_Wk"], dtype=np.float32)
    Wv = np.asarray(inp["L0_attn_Wv"], dtype=np.float32)
    Wo = np.asarray(inp["L0_attn_Wo"], dtype=np.float32)
    bo = np.asarray(inp["L0_attn_bo"], dtype=np.float32)
    fWq = np.asarray(inp["L0_ffn_Wq"], dtype=np.float32)
    fWr = np.asarray(inp["L0_ffn_Wr"], dtype=np.float32)
    fak = np.asarray(inp["L0_ffn_aux_keys"], dtype=np.float32)
    fav = np.asarray(inp["L0_ffn_aux_values"], dtype=np.float32)
    fWvd = np.asarray(inp["L0_ffn_Wvd"], dtype=np.float32)
    fWvu = np.asarray(inp["L0_ffn_Wvu"], dtype=np.float32)
    fWu1 = np.asarray(inp["L0_ffn_Wu1"], dtype=np.float32)
    fbu1 = np.asarray(inp["L0_ffn_bu1"], dtype=np.float32)
    fWu2 = np.asarray(inp["L0_ffn_Wu2"], dtype=np.float32)
    fbu2 = np.asarray(inp["L0_ffn_bu2"], dtype=np.float32)
    gw1 = np.asarray(inp["L0_ffn_gate_w1"], dtype=np.float32)
    gbi = np.asarray(inp["L0_ffn_gate_bias"], dtype=np.float32)
    flr = np.asarray(inp["L0_ffn_log_rel"], dtype=np.float32)

    C = np.ascontiguousarray
    wq_h = C(Wq.reshape(32, 128, 32, 128).transpose(0, 3, 2, 1).reshape(HQ, 128, E))
    wk_h = C(Wk.reshape(8, 128, 32, 128).transpose(0, 3, 2, 1).reshape(HKV, 128, E))
    wv_h = C(Wv.reshape(8, 128, 32, 128).transpose(0, 3, 2, 1).reshape(HKV, 128, E))
    wo_h = C(Wo.reshape(8, 512, 32, 128).transpose(0, 3, 2, 1).reshape(8, 128, 32 * 512))
    wu1_h = C(fWu1.reshape(8, 128, 32, 128).transpose(0, 3, 2, 1).reshape(8, 128, E))
    av_h = C(fav.reshape(8, 128, 32, 128).transpose(2, 1, 0, 3).reshape(32, 128, 8 * 128))
    wvd_h = C(fWvd.T.reshape(32, 128, VB))
    wvu_h = C(fWvu.reshape(8, 512, 2, 128).transpose(0, 2, 3, 1))
    wqf_h = C(fWq.reshape(BOT, 32, 128).transpose(2, 1, 0).reshape(128, 32 * BOT))
    wr_h = C(fWr.T)
    ak_h = C(fak.T)
    wu2_h = C(fWu2.reshape(8, 128).T)
    bu1_h = C(fbu1.reshape(8, 128).T)
    bu2_h = fbu2.reshape(1, 1)
    bo_h = bo.reshape(1, E)
    lr_h = flr.reshape(1, NSLOT)
    sc_h = np.array([[float(gw1), float(gbi)]], dtype=np.float32)
    eye_h = np.eye(128, dtype=np.float32)

    in_maps = []
    for c in range(8):
        b, q4 = c // 4, c % 4
        xo = np.roll(x[b], -q4 * 1024, axis=0)
        in_maps.append({
            "x_own": C(xo.reshape(NCH, 128, E)),
            "x_oth": C(x[1 - b].reshape(NCH, 128, E)),
            "wq_f": wq_h, "wk_f": wk_h, "wv_f": wv_h, "wo_f": wo_h,
            "wu1_f": wu1_h, "av_f": av_h, "wvd_f": wvd_h, "wvu_f": wvu_h,
            "wqf_f": wqf_h, "wr_f": wr_h, "ak_f": ak_h, "wu2_f": wu2_h,
            "bu1_f": bu1_h, "bu2_f": bu2_h, "bo_f": bo_h, "lr_f": lr_h,
            "sc_f": sc_h, "eye_f": eye_h,
        })

    nc = build_nc()
    res = run_bass_kernel_spmd(nc, in_maps, core_ids=list(range(8)))
    global _LAST_RES
    _LAST_RES = res
    out = np.empty((B, T, E), dtype=np.float32)
    for c in range(8):
        b, q4 = c // 4, c % 4
        out[b, q4 * 1024:(q4 + 1) * 1024] = res.results[c]["y"].reshape(1024, E)
    return out


